# version 11
# baseline (speedup 1.0000x reference)
import numpy as np
import ml_dtypes
import concourse.bass as bass
import concourse.mybir as mybir
from concourse.bass_utils import run_bass_kernel_spmd

F32 = mybir.dt.float32
BF16 = mybir.dt.bfloat16
AF = mybir.ActivationFunctionType
ALU = mybir.AluOpType
AX = mybir.AxisListType
BF = ml_dtypes.bfloat16

HID = 2048
INTER = 4096
CONVD = 6144
PROJ = 10304
NH, PDIM, NG, NST, CS = 64, 64, 8, 128, 256
LSEQ = 4096
TILE = 512
EXP, HEXP, HSH = 8, 1408, 2816
EPS = 1e-5

N_TILES = LSEQ // TILE
DEBUG_TAPS = []

import bisect


class TK:

    def __init__(self, nc):
        self.nc = nc
        self.E = {"pe": nc.tensor, "dve": nc.vector, "act": nc.scalar,
                  "pool": nc.gpsimd, "sp": nc.sync}
        self.QENG = {"qsp": "sp", "qact": "act", "qpool": "pool"}
        self.sem = {k: nc.alloc_semaphore("sem_" + k)
                    for k in ["pe", "dve", "act", "pool"]}
        self.n = {k: 0 for k in self.E}
        self.insts = {k: [] for k in self.E}
        self.isdma = {k: [] for k in self.E}
        self.incs = {k: [] for k in ["pe", "dve", "act", "pool"]}
        self.qcnt = {}
        self.lastw = {}
        self.readers = {}
        self.waited = {k: {} for k in self.E}

    def chan(self, name):
        if name not in self.qcnt:
            self.qcnt[name] = 0
            self.sem[name] = self.nc.alloc_semaphore("sem_" + name)
        return name

    def _wait(self, stream, semk, val):
        if val <= self.waited[stream].get(semk, 0):
            return
        self.E[stream].wait_ge(self.sem[semk], val)
        self.waited[stream][semk] = val

    def _need(self, stream, src):
        if src[0] == "q":
            _, ch, cnt = src
            self._wait(stream, ch, 16 * cnt)
        else:
            _, eng, idx = src
            if eng == stream and eng == "pe":
                return
            arr = self.incs[eng]
            i = bisect.bisect_left(arr, (idx, -1))
            if i < len(arr):
                pos, cum = arr[i]
            else:
                assert not self.isdma[eng][idx], "inc attach on dma instr"
                h = self.insts[eng][idx]
                cum = (arr[-1][1] + 1) if arr else 1
                h.then_inc(self.sem[eng], 1)
                arr.append((idx, cum))
            self._wait(stream, eng, cum)

    def _pre(self, stream, reads, writes):
        for k in reads:
            lw = self.lastw.get(k)
            if lw is not None:
                self._need(stream, lw)
        for k in writes:
            lw = self.lastw.get(k)
            if lw is not None:
                self._need(stream, lw)
            for src in self.readers.get(k, {}).values():
                self._need(stream, src)

    def _post(self, me, srcname, reads, writes):
        for k in reads:
            self.readers.setdefault(k, {})[srcname] = me
        for k in writes:
            self.lastw[k] = me
            self.readers[k] = {}

    def op(self, stream, emit, reads=(), writes=()):
        self._pre(stream, reads, writes)
        h = emit(self.E[stream])
        idx = self.n[stream]
        self.insts[stream].append(h)
        self.isdma[stream].append(False)
        self.n[stream] += 1
        self._post(("e", stream, idx), stream, reads, writes)
        return h

    def dma(self, q, out_ap, in_ap, reads=(), writes=(), ch=None, **kw):
        stream = self.QENG[q]
        assert ch is not None
        self.chan(ch)
        self._pre(stream, reads, writes)
        h = self.E[stream].dma_start(out_ap, in_ap, **kw)
        self.insts[stream].append(h)
        self.isdma[stream].append(True)
        self.n[stream] += 1
        self.qcnt[ch] += 1
        h.then_inc(self.sem[ch], 16)
        self._post(("q", ch, self.qcnt[ch]), ch, reads, writes)
        return h

    def alias(self, dst, *srcs):
        best = None
        rd = {}
        for s in srcs:
            lw = self.lastw.get(s)
            if lw is not None:
                self.lastw.setdefault(dst, None)
                rd[("w",) + lw] = lw
            for name, src in self.readers.get(s, {}).items():
                rd[("r", name, s)] = src
        if rd:
            d = self.readers.setdefault(dst, {})
            for kk, src in rd.items():
                d[kk] = src

    def finish(self):
        for ch in list(self.qcnt):
            self._wait("sp", ch, 16 * self.qcnt[ch])
        for eng in ["pe", "dve", "act", "pool"]:
            i = self.n[eng] - 1
            while i >= 0 and self.isdma[eng][i]:
                i -= 1
            if i >= 0:
                self._need("sp", ("e", eng, i))




def _tile2d(w, rows, cols, rtile=128, ctile=128):
    rt, ct = rows // rtile, cols // ctile
    g = w.reshape(rt, rtile, ct, ctile).transpose(2, 1, 0, 3)
    return np.ascontiguousarray(g.reshape(ct * rtile, rt * ctile))


def prep_inputs(ins):
    d = {}
    f32 = np.float32
    W_inT = np.ascontiguousarray(ins["L0_attn_in_proj_w"].T)
    d["w_in"] = _tile2d(W_inT[:, :10240], 2048, 10240)
    d["w_indt"] = _tile2d(W_inT[:, 10240:], 2048, 64, ctile=64)
    W_outT = np.ascontiguousarray(ins["L0_attn_out_proj_w"].T)
    d["w_out"] = _tile2d(W_outT, 4096, 2048)

    wg, wu, wd = ins["L0_ffn_w_gate"], ins["L0_ffn_w_up"], ins["L0_ffn_w_down"]
    gu = []
    for e in range(EXP):
        tg = _tile2d(wg[e], 2048, 1408)
        tu = _tile2d(wu[e], 2048, 1408)
        gu.append(np.concatenate(
            [tg.reshape(11, 128, 2048), tu.reshape(11, 128, 2048)], axis=2))
    d["w_gu"] = np.ascontiguousarray(
        np.stack(gu).reshape(EXP * 11 * 128, 4096))
    wdt = []
    for e in range(EXP):
        t = _tile2d(wd[e], 1408, 2048)
        wdt.append(t.reshape(4, 4, 128, 1408).transpose(0, 2, 1, 3).reshape(4 * 128, 4 * 1408))
    d["w_d"] = np.ascontiguousarray(np.stack(wdt).reshape(EXP * 4 * 128, 4 * 1408))

    tsg = _tile2d(ins["L0_ffn_sw_gate"], 2048, 2816)
    tsu = _tile2d(ins["L0_ffn_sw_up"], 2048, 2816)
    d["w_sgu"] = np.ascontiguousarray(np.concatenate(
        [tsg.reshape(22, 128, 2048), tsu.reshape(22, 128, 2048)], axis=2
    ).reshape(22 * 128, 4096))
    tsd = _tile2d(ins["L0_ffn_sw_down"], 2816, 2048)
    d["w_sd"] = np.ascontiguousarray(
        tsd.reshape(8, 2, 128, 2816).transpose(0, 2, 1, 3).reshape(8 * 128, 2 * 2816))

    d["w_gate"] = np.ascontiguousarray(
        ins["L0_ffn_gate_w"].reshape(16, 128, 8).transpose(1, 0, 2).reshape(128, 128))

    d["conv_w_t"] = np.ascontiguousarray(
        ins["L0_attn_conv_w"].reshape(48, 128, 4).transpose(1, 0, 2).reshape(128, 192))
    d["conv_b_t"] = np.ascontiguousarray(ins["L0_attn_conv_b"].reshape(48, 128).T)
    d["norm_w_t"] = np.ascontiguousarray(ins["L0_attn_norm_w"].reshape(32, 128).T)
    d["dt_bias_c"] = ins["L0_attn_dt_bias"].reshape(64, 1).astype(f32)
    d["a_log_c"] = ins["L0_attn_A_log"].reshape(64, 1).astype(f32)
    pp = np.arange(128)[:, None, None] // 64
    gg = np.arange(8)[None, :, None]
    rr = np.arange(4)[None, None, :]
    idx = gg * 8 + 2 * rr + pp
    d["d_exp"] = np.ascontiguousarray(
        ins["L0_attn_D"][idx.reshape(128, 32)]).astype(f32)

    d["id128"] = np.eye(128, dtype=f32)
    d["id128b"] = np.eye(128, dtype=f32).astype(BF)
    d["ones1b"] = np.ones((1, 128), dtype=f32).astype(BF)
    d["ones1f"] = np.ones((1, 128), dtype=f32)
    d["onescolb"] = np.ones((128, 1), dtype=f32).astype(BF)
    sel8 = np.zeros((8, 8 * 128), dtype=f32)
    for h in range(8):
        sel8[h, h * 128:(h + 1) * 128] = 1.0
    d["sel8"] = sel8.astype(BF)
    selp = np.zeros((8, 4 * 128), dtype=f32)
    for rc in range(4):
        selp[2 * rc, rc * 128:rc * 128 + 64] = 1.0
        selp[2 * rc + 1, rc * 128 + 64:rc * 128 + 128] = 1.0
    d["selpair"] = selp.astype(BF)
    mask = np.zeros((128, 512), dtype=f32)
    for jc in range(2):
        for p in range(128):
            for i in range(256):
                if i < jc * 128 + p:
                    mask[p, jc * 256 + i] = -1e30
    d["maskneg"] = mask.astype(BF)
    d["zcol"] = np.zeros((128, 1), dtype=f32)
    d["epscol"] = np.full((128, 1), EPS, dtype=f32)

    xs = [np.ascontiguousarray(ins["L0_attn_hidden_states"][c]) for c in range(8)]
    return d, xs




def build_program(n_tiles=N_TILES, taps=(), stage="full"):
    nc = bass.Bass("TRN2", target_bir_lowering=False)
    tk = TK(nc)
    for _c in ["cc", "ci", "cid", "co", "cgu", "cd", "csg", "csd", "cx",
               "cwa", "cwb", "cda", "cdb", "cza", "czb", "czs", "cout", "ctap"]:
        tk.chan(_c)
    for _s in tk.sem.values():
        nc.gpsimd.sem_clear(_s)
    nc.all_engine_barrier()

    def din(name, shape, dt=F32):
        return nc.dram_tensor(name, list(shape), dt, kind="ExternalInput").ap()

    xin = din("xin", (LSEQ, HID))
    w_in = din("w_in", (80 * 128, 2048))
    w_indt = din("w_indt", (128, 1024))
    w_out = din("w_out", (16 * 128, 4096))
    w_gu = din("w_gu", (EXP * 11 * 128, 4096))
    w_d = din("w_d", (EXP * 4 * 128, 4 * 1408))
    w_sgu = din("w_sgu", (22 * 128, 4096))
    w_sd = din("w_sd", (8 * 128, 2 * 2816))
    w_gate = din("w_gate", (128, 128))
    conv_w_t = din("conv_w_t", (128, 192))
    conv_b_t = din("conv_b_t", (128, 48))
    norm_w_t = din("norm_w_t", (128, 32))
    dt_bias_c = din("dt_bias_c", (64, 1))
    a_log_c = din("a_log_c", (64, 1))
    d_exp = din("d_exp", (128, 32))
    id128 = din("id128", (128, 128))
    id128b = din("id128b", (128, 128), BF16)
    ones1b = din("ones1b", (1, 128), BF16)
    ones1f = din("ones1f", (1, 128))
    onescolb = din("onescolb", (128, 1), BF16)
    sel8 = din("sel8", (8, 1024), BF16)
    selpair = din("selpair", (8, 512), BF16)
    maskneg = din("maskneg", (128, 512), BF16)
    zcol = din("zcol", (128, 1))
    epscol = din("epscol", (128, 1))

    out = nc.dram_tensor("out", [LSEQ, HID], F32, kind="ExternalOutput").ap()

    def scr(name, shape):
        return nc.dram_tensor(name, list(shape), BF16, kind="Internal").ap()

    bw_in = scr("bw_in", (80 * 128, 2048))
    bw_indt = scr("bw_indt", (128, 1024))
    bw_out = scr("bw_out", (16 * 128, 4096))
    bw_gu = scr("bw_gu", (EXP * 11 * 128, 4096))
    bw_d = scr("bw_d", (EXP * 4 * 128, 4 * 1408))
    bw_sgu = scr("bw_sgu", (22 * 128, 4096))
    bw_sd = scr("bw_sd", (8 * 128, 2 * 2816))
    zscr = scr("zscr", (32 * 128, 512))

    def sb(name, shape, dt):
        return nc.alloc_sbuf_tensor(name, list(shape), dt)

    x_tm = sb("x_tm", (128, 4 * 2048), F32)
    fmbuf = sb("fmbuf", (128, 8192), BF16)
    big1 = sb("big1", (128, 8192), F32)
    b_fm = sb("b_fm", (128, 4096), BF16)
    c_fm = sb("c_fm", (128, 4096), BF16)
    big2 = sb("big2", (128, 2816), F32)
    big3 = sb("big3", (128, 8192), F32)
    S = sb("S", (128, 4096), F32)
    sbf_g = sb("sbf_g", (128, 512), BF16)
    wbufA = sb("wbufA", (128, 4096), BF16)
    wbufB = sb("wbufB", (128, 4096), BF16)
    xbcs0 = sb("xbcs0", (128, 516), BF16)
    xbcs1 = sb("xbcs1", (128, 516), BF16)
    convacc0 = sb("convacc0", (128, 512), BF16)
    convacc1 = sb("convacc1", (128, 512), BF16)
    tail = sb("tail", (128, 144), BF16)
    dts = sb("dts", (64, 512), F32)
    dAcs = sb("dAcs", (64, 256), F32)
    dAcsT = sb("dAcsT", (128, 128), F32)
    colpT = sb("colpT", (128, 128), F32)
    wT = sb("wT", (128, 128), F32)
    dAcs8h = sb("dAcs8h", (8, 512), BF16)
    dAcs8m = sb("dAcs8m", (8, 512), BF16)
    dAcs8l = sb("dAcs8l", (8, 512), BF16)
    e8g = sb("e8g", (8, 256), BF16)
    decrow = sb("decrow", (1, 64), F32)
    decayexp = sb("decayexp", (128, 64), F32)
    cbT = sb("cbT", (128, 512), BF16)
    bT = sb("bT", (128, 256), BF16)
    wx = sb("wx", (128, 1024), BF16)
    expb0 = sb("expb0", (128, 256), BF16)
    expb1 = sb("expb1", (128, 256), BF16)
    mT0 = sb("mT0", (128, 256), BF16)
    mT1 = sb("mT1", (128, 256), BF16)
    eb_s = sb("eb_s", (128, 256), BF16)
    ytmp = sb("ytmp", (128, 256), F32)
    ysq0 = sb("ysq0", (128, 256), BF16)
    ysq1 = sb("ysq1", (128, 256), BF16)
    varrow = sb("varrow", (1, 256), F32)
    rstdb = sb("rstdb", (1, 256), BF16)
    rb_s = sb("rb_s", (128, 256), BF16)
    os0 = sb("os0", (128, 256), BF16)
    os1 = sb("os1", (128, 256), BF16)
    colm = sb("colm", (128, 1), F32)
    colv = sb("colv", (128, 1), F32)
    colt = sb("colt", (128, 1), F32)
    zg4a = sb("zg4a", (128, 512), BF16)
    zg4b = sb("zg4b", (128, 512), BF16)
    glog = sb("glog", (128, 8), F32)
    gsc = sb("gsc", (128, 8), F32)
    gtmp = sb("gtmp", (128, 8), F32)
    gm1 = sb("gm1", (128, 1), F32)
    comb_fm = sb("comb_fm", (8, 512), BF16)
    c_convw = sb("c_convw", (128, 192), F32)
    c_convb = sb("c_convb", (128, 48), F32)
    c_normw = sb("c_normw", (128, 32), F32)
    c_dtb = sb("c_dtb", (64, 1), F32)
    c_acol = sb("c_acol", (64, 1), F32)
    c_dexp = sb("c_dexp", (128, 32), F32)
    c_id = sb("c_id", (128, 128), F32)
    c_idb = sb("c_idb", (128, 128), BF16)
    c_ones1b = sb("c_ones1b", (1, 128), BF16)
    c_ones1f = sb("c_ones1f", (1, 128), F32)
    c_onescolb = sb("c_onescolb", (128, 1), BF16)
    c_sel8 = sb("c_sel8", (8, 1024), BF16)
    c_selpair = sb("c_selpair", (8, 512), BF16)
    c_mask = sb("c_mask", (128, 512), BF16)
    c_zcol = sb("c_zcol", (128, 1), F32)
    c_eps = sb("c_eps", (128, 1), F32)
    c_gateb = sb("c_gateb", (128, 128), BF16)

    big1b = big1[:].bitcast(BF16)
    big2b = big2[:].bitcast(BF16)
    big3b = big3[:].bitcast(BF16)

    pb = [nc.alloc_psum_tensor(f"pb{i}", [128, 512], F32) for i in range(8)]
    pbb = [p[:].bitcast(BF16) for p in pb]

    lnstage = big3b[:, 8192:10240]
    zstage0, zstage1 = convacc0, convacc1
    dA = big3[0:64, 2560:2816]
    lndt = big3[0:64, 2816:3072]
    w_hm = ytmp[0:64, 0:256]

    def K(ap_or_name):
        return ap_or_name if isinstance(ap_or_name, str) else ap_or_name.name

    def mm(bank, lhsT, rhs, start, stop, rkeys, out_sl=None):
        o = pb[bank][:] if out_sl is None else out_sl
        tk.op("pe", lambda e: e.matmul(o, lhsT, rhs, start=start, stop=stop),
              reads=tuple(rkeys), writes=(f"pb{bank}",))

    def tp(bank, in_ap, ident, rkeys, out_sl):
        tk.op("pe", lambda e: e.transpose(out_sl, in_ap, ident),
              reads=tuple(rkeys), writes=(f"pb{bank}",))

    taps_out = {}

    def tap(name, ap, shape, dt=F32):
        if name not in taps:
            return
        t = nc.dram_tensor("tap_" + name, list(shape), dt, kind="ExternalOutput").ap()
        tk._wait("act", "ctap", 16 * tk.qcnt["ctap"])
        tk.dma("qact", t, ap, reads=(ap.name,), ch="ctap")
        taps_out[name] = True

    for dst, src in [
        (c_convw[:], conv_w_t), (c_convb[:], conv_b_t), (c_normw[:], norm_w_t),
        (c_dtb[:], dt_bias_c), (c_dexp[:], d_exp), (c_id[:], id128),
        (c_idb[:], id128b), (c_ones1b[:], ones1b), (c_ones1f[:], ones1f),
        (c_onescolb[:], onescolb), (c_sel8[:], sel8), (c_selpair[:], selpair),
        (c_mask[:], maskneg), (c_zcol[:], zcol), (c_eps[:], epscol),
    ]:
        tk.dma("qsp", dst, src, writes=(dst.name,), ch="cc")
    tk.dma("qsp", colt[0:64, :], a_log_c, writes=("colt",), ch="cc")
    tk.dma("qsp", big3[:, 0:128], w_gate, writes=("big3",), ch="cc")
    _cq = 16 * tk.qcnt["cc"]
    for _eng in ["pe", "dve", "act", "pool", "sp"]:
        tk._wait(_eng, "cc", _cq)
    tk.op("act", lambda e: e.activation(c_acol[0:64, :], colt[0:64, :], AF.Exp),
          reads=("colt",), writes=("c_acol",))
    tk.op("dve", lambda e: e.tensor_scalar_mul(c_acol[0:64, :], c_acol[0:64, :], -1.0),
          reads=("c_acol",), writes=("c_acol",))
    tk.op("dve", lambda e: e.tensor_copy(c_gateb[:], big3[:, 0:128]),
          reads=("big3",), writes=("c_gateb",))
    tk.op("dve", lambda e: e.memset(S[:], 0.0), writes=("S",))
    tk.op("dve", lambda e: e.memset(tail[:], 0.0), writes=("tail",))

    def cast_all(ch, dst, src, rows, chunk_rows=1024):
        r = 0
        while r < rows:
            rr = min(chunk_rows, rows - r)
            tk.dma("qpool", dst[r:r + rr, :], src[r:r + rr, :],
                   reads=(src.name,), writes=(dst.name,), ch=ch)
            r += rr

    if stage != "ln1_nocast":
        cast_all("ci", bw_in, w_in, 80 * 128, 2048)
        cast_all("cid", bw_indt, w_indt, 128)
        cast_all("co", bw_out, w_out, 16 * 128, 1024)
        cast_all("csg", bw_sgu, w_sgu, 22 * 128, 1024)
        cast_all("cgu", bw_gu, w_gu, EXP * 11 * 128, 1024)
        cast_all("cd", bw_d, w_d, EXP * 4 * 128, 512)
        cast_all("csd", bw_sd, w_sd, 8 * 128, 512)

    evac_rr = [0]

    def evac(dst_ap, src_ap, rkeys, wkeys):
        eng = "dve" if evac_rr[0] % 2 == 0 else "act"
        evac_rr[0] += 1
        if eng == "dve":
            tk.op("dve", lambda e: e.tensor_copy(dst_ap, src_ap),
                  reads=tuple(rkeys), writes=tuple(wkeys))
        else:
            tk.op("act", lambda e: e.copy(dst_ap, src_ap),
                  reads=tuple(rkeys), writes=tuple(wkeys))

    def layernorm_block(b, sq_scr, sq_key):
        xb = x_tm[:, b * 2048:(b + 1) * 2048]
        tk.op("dve", lambda e: e.reduce_sum(colm[:], xb, AX.X),
              reads=("x_tm",), writes=("colm",))
        tk.op("act", lambda e: e.activation(sq_scr, xb, AF.Square, accum_out=colv[:]),
              reads=("x_tm",), writes=(sq_key, "colv"))
        tk.op("dve", lambda e: e.tensor_scalar_mul(colm[:], colm[:], 1.0 / 2048),
              reads=("colm",), writes=("colm",))
        tk.op("dve", lambda e: e.tensor_scalar_mul(colv[:], colv[:], 1.0 / 2048),
              reads=("colv",), writes=("colv",))
        tk.op("dve", lambda e: e.tensor_mul(colt[:], colm[:], colm[:]),
              reads=("colm",), writes=("colt",))
        tk.op("dve", lambda e: e.tensor_sub(colv[:], colv[:], colt[:]),
              reads=("colv", "colt"), writes=("colv",))
        tk.op("dve", lambda e: e.tensor_scalar(colv[:], colv[:], c_eps[:], None,
                                               ALU.add),
              reads=("colv", "c_eps"), writes=("colv",))
        tk.op("act", lambda e: e.activation(colv[:], colv[:], AF.Ln),
              reads=("colv",), writes=("colv",))
        tk.op("act", lambda e: e.activation(colv[:], colv[:], AF.Exp, scale=-0.5),
              reads=("colv",), writes=("colv",))
        tk.op("dve", lambda e: e.tensor_scalar(lnstage, xb, colm[:], colv[:],
                                               ALU.subtract, ALU.mult),
              reads=("x_tm", "colm", "colv"), writes=("big3",))

    def transpose_to_fm(dst_col_of_chunk, b, tbank):
        for half in range(2):
            bk = tbank if (b * 2 + half) % 2 == 0 else 7
            for k in range(8):
                c = half * 8 + k
                tp(bk, lnstage[:, c * 128:(c + 1) * 128], c_idb[:],
                   ("big3",), pbb[bk][:, k * 128:(k + 1) * 128])
            dst = fmbuf[:].rearrange("p (c t) -> p c t", t=512)[
                :, half * 8:half * 8 + 8, b * 128:(b + 1) * 128]
            tk.op("act", lambda e, d=dst, s=pbb[bk][:, 0:1024]: e.copy(d, s),
                  reads=(f"pb{bk}",), writes=("fmbuf",))

    sqv = big3[:, 0:2048]

    for t in range(n_tiles):
        T0 = t * TILE
        tk.alias("big3", "wdA", "wdB", "yn0", "yn1", "yn2", "yn3", "yn4", "yn5", "yn6", "yn7", "yn8", "yn9", "yn10", "yn11", "yn12", "yn13", "yn14", "yn15", "yn16", "yn17", "yn18", "yn19", "yn20", "yn21", "yn22", "yn23", "yn24", "yn25", "yn26", "yn27", "yn28", "yn29", "yn30", "yn31")
        for b in range(4):
            tk.dma("qact", x_tm[:, b * 2048:(b + 1) * 2048],
                   xin[T0 + b * 128:T0 + (b + 1) * 128, :],
                   reads=("xin",), writes=("x_tm",), ch="cx")
        if t == 0:
            tap("xin0", x_tm[:], (128, 8192), F32)
        for b in range(4):
            layernorm_block(b, sqv, "big3")
            if t == 0 and b == 0:
                tap("colm0", colm[:], (128, 1), F32)
                tap("colv0", colv[:], (128, 1), F32)
                tap("lnst0", lnstage, (128, 2048), BF16)
            transpose_to_fm(None, b, tbank=6)
        if t == 0:
            tap("lnx0", fmbuf[:], (128, 8192), BF16)
        if stage in ("ln1", "ln1_nocast"):
            break


        def stream_in(gi):
            buf = wbufA if gi % 2 == 0 else wbufB
            tk.dma("qsp", buf[:].rearrange("p (o q) -> p o q", o=2),
                   bw_in[gi * 2 * 128:(gi * 2 + 2) * 128, :]
                   .rearrange("(o p) q -> p o q", p=128),
                   reads=("bw_in",), writes=(buf.name,),
                   ch="cwa" if gi % 2 == 0 else "cwb")
            return buf

        stream_in(0)
        stream_in(1)
        conv_par = [0]
        for oc in range(80):
            gi = oc // 2
            buf = wbufA if gi % 2 == 0 else wbufB
            off = (oc % 2) * 2048
            bank = oc % 2
            for dc in range(16):
                mm(bank, buf[:, off + dc * 128:off + (dc + 1) * 128],
                   fmbuf[:, dc * 512:(dc + 1) * 512],
                   start=(dc == 0), stop=(dc == 15),
                   rkeys=(buf.name, "fmbuf"))
            if oc < 32:
                zs = zstage0 if oc % 2 == 0 else zstage1
                evac(zs[:], pb[bank][:], (f"pb{bank}",), (zs.name,))
                tk.dma("qact", zscr[oc * 128:(oc + 1) * 128, :], zs[:],
                       reads=(zs.name,), writes=("zscr",), ch="czs")
            else:
                c48 = oc - 32
                xs = xbcs0 if c48 % 2 == 0 else xbcs1
                tk.op("dve", lambda e, x_=xs: e.tensor_copy(x_[:, 3:515], pb[bank][:]),
                      reads=(f"pb{bank}",), writes=(xs.name,))
                tk.op("dve", lambda e, x_=xs: e.tensor_copy(
                    x_[:, 0:3], tail[:, c48 * 3:c48 * 3 + 3]),
                    reads=("tail",), writes=(xs.name,))
                tk.op("dve", lambda e, x_=xs: e.tensor_copy(
                    tail[:, c48 * 3:c48 * 3 + 3], x_[:, 512:515]),
                    reads=(xs.name,), writes=("tail",))
                acc, tmp = convacc0, convacc1
                wcol = lambda k: c_convw[:, c48 * 4 + k:c48 * 4 + k + 1].broadcast_to([128, 512])
                tk.op("pool", lambda e, x_=xs: e.tensor_tensor(
                    acc[:], x_[:, 0:512], wcol(0), ALU.mult),
                    reads=(xs.name, "c_convw"), writes=("convacc0",))
                for kk in range(1, 4):
                    tk.op("pool", lambda e, x_=xs, k_=kk: e.tensor_tensor(
                        tmp[:], x_[:, k_:k_ + 512], wcol(k_), ALU.mult),
                        reads=(xs.name, "c_convw"), writes=("convacc1",))
                    tk.op("pool", lambda e: e.tensor_tensor(
                        acc[:], acc[:], tmp[:], ALU.add),
                        reads=("convacc0", "convacc1"), writes=("convacc0",))
                tk.op("pool", lambda e: e.tensor_tensor(
                    acc[:], acc[:],
                    c_convb[:, c48:c48 + 1].broadcast_to([128, 512]), ALU.add),
                    reads=("convacc0", "c_convb"), writes=("convacc0",))
                ca = acc
                if c48 < 32:
                    dst = big1b[:, c48 * 512:(c48 + 1) * 512]
                    dk = "big1"
                elif c48 < 40:
                    g = c48 - 32
                    dst = b_fm[:, g * 512:(g + 1) * 512]
                    dk = "b_fm"
                else:
                    g = c48 - 40
                    dst = c_fm[:, g * 512:(g + 1) * 512]
                    dk = "c_fm"
                tk.op("act", lambda e, d_=dst, a_=ca: e.activation(d_, a_[:], AF.Silu),
                      reads=(ca.name,), writes=(dk,))
            if oc % 2 == 1 and gi + 2 < 40:
                stream_in(gi + 2)
        tk.dma("qsp", wbufA[:, 0:1024], bw_indt[:, :], reads=("bw_indt",),
               writes=("wbufA",), ch="cwa")
        for dc in range(16):
            mm(2, wbufA[:, dc * 64:(dc + 1) * 64], fmbuf[:, dc * 512:(dc + 1) * 512],
               start=(dc == 0), stop=(dc == 15), rkeys=("wbufA", "fmbuf"),
               out_sl=pb[2][0:64, :])
        tk.op("dve", lambda e: e.tensor_scalar(dts[:], pb[2][0:64, :], c_dtb[:],
                                               None, ALU.add),
              reads=("pb2", "c_dtb"), writes=("dts",))
        dtab = big3[0:64, 2048:2560]
        tk.op("act", lambda e: e.activation(dtab, dts[:], AF.Abs),
              reads=("dts",), writes=("big3",))
        tk.op("act", lambda e: e.activation(dtab, dtab, AF.Exp, scale=-1.0),
              reads=("big3",), writes=("big3",))
        tk.op("act", lambda e: e.activation(dtab, dtab, AF.Ln, bias=1.0),
              reads=("big3",), writes=("big3",))
        tk.op("dve", lambda e: e.tensor_scalar_max(dts[:], dts[:], 0.0),
              reads=("dts",), writes=("dts",))
        tk.op("dve", lambda e: e.tensor_add(dts[:], dts[:], dtab),
              reads=("dts", "big3"), writes=("dts",))
        if t == 0:
            tap("z0", zscr, (32 * 128, 512), BF16)
            tap("xfm0", big1b[:, 0:16384], (128, 16384), BF16)
            tap("bfm0", b_fm[:], (128, 4096), BF16)
            tap("cfm0", c_fm[:], (128, 4096), BF16)
            tap("dts0", dts[:], (64, 512), F32)
        if stage == "inproj":
            break

        for cc in range(2):
            csl = slice(cc * 256, (cc + 1) * 256)
            tk.op("dve", lambda e: e.tensor_scalar(dA, dts[:, csl], c_acol[0:64, :],
                                                   None, ALU.mult),
                  reads=("dts", "c_acol"), writes=("big3",))
            tk.op("dve", lambda e: e.tensor_tensor_scan(
                dAcs[:], dA, c_zcol[0:64, 0:1].broadcast_to([64, 256]), 0.0,
                ALU.add, ALU.add),
                reads=("big3", "c_zcol"), writes=("dAcs",))
            tk.op("act", lambda e: e.activation(lndt, dts[:, csl], AF.Ln),
                  reads=("dts",), writes=("big3",))
            tk.op("dve", lambda e: e.tensor_sub(lndt, lndt, dAcs[:]),
                  reads=("big3", "dAcs"), writes=("big3",))
            tk.op("act", lambda e: e.activation(w_hm, dAcs[:], AF.Exp, scale=-1.0,
                                                bias=dAcs[:, 255:256]),
                  reads=("dAcs",), writes=("ytmp",))
            tk.op("dve", lambda e: e.tensor_mul(w_hm, w_hm, dts[:, csl]),
                  reads=("ytmp", "dts"), writes=("ytmp",))
            for (srcT, dstT, skey, key) in (
                    (dAcs[:], dAcsT, "dAcs", "dAcsT"),
                    (lndt, colpT, "big3", "colpT"),
                    (w_hm, wT, "ytmp", "wT")):
                for jc in range(2):
                    tp(7, srcT[:, jc * 128:(jc + 1) * 128], c_id[0:64, 0:64],
                       (skey,), pb[7][0:128, 0:64])
                    evac(dstT[:, jc * 64:(jc + 1) * 64], pb[7][0:128, 0:64],
                         ("pb7",), (key,))
            tp(7, dAcs[:, 255:256], c_id[0:64, 0:64], ("dAcs",), pb[7][0:1, 0:64])
            tk.op("dve", lambda e: e.tensor_copy(decrow[:], pb[7][0:1, 0:64]),
                  reads=("pb7",), writes=("decrow",))
            tk.op("act", lambda e: e.activation(decrow[:], decrow[:], AF.Exp),
                  reads=("decrow",), writes=("decrow",))
            tk.op("pe", lambda e: e.matmul(pb[7][0:128, 0:64], c_ones1f[:], decrow[:],
                                           start=True, stop=True),
                  reads=("c_ones1f", "decrow"), writes=("pb7",))
            tk.op("dve", lambda e: e.tensor_copy(decayexp[:], pb[7][0:128, 0:64]),
                  reads=("pb7",), writes=("decayexp",))
            for jc in range(2):
                for half in range(4):
                    bk = 6 if (jc * 4 + half) % 2 == 0 else 7
                    for k in range(8):
                        c = half * 8 + k
                        tp(bk, big1b[:, c * 512 + cc * 256 + jc * 128:
                                     c * 512 + cc * 256 + (jc + 1) * 128],
                           c_idb[:], ("big1",), pbb[bk][:, k * 128:(k + 1) * 128])
                    dst = fmbuf[:, jc * 4096 + half * 1024:jc * 4096 + (half + 1) * 1024]
                    tk.op("act", lambda e, d_=dst, b_=bk: e.copy(d_, pbb[b_][:, 0:1024]),
                          reads=(f"pb{bk}",), writes=("fmbuf",))

            for g in range(NG):
                gsl = slice(g * 512, (g + 1) * 512)
                dAcs8 = ytmp[0:8, 0:256]
                for jc in range(2):
                    tp(5, dAcsT[:, jc * 64 + g * 8:jc * 64 + (g + 1) * 8],
                       c_id[:], ("dAcsT",), pb[5][0:8, 0:128])
                    evac(dAcs8[:, jc * 128:(jc + 1) * 128], pb[5][0:8, 0:128],
                         ("pb5",), ("ytmp",))
                tk.op("act", lambda e: e.activation(e8g[:], dAcs8, AF.Exp),
                      reads=("ytmp",), writes=("e8g",))
                dup = lambda ap: ap.unsqueeze(1).broadcast_to([8, 2, 256])
                tk.op("dve", lambda e: e.tensor_copy(
                    dAcs8h[:].rearrange("p (r i) -> p r i", r=2), dup(dAcs8)),
                    reads=("ytmp",), writes=("dAcs8h",))
                tk.op("dve", lambda e: e.tensor_sub(dAcs8, dAcs8, dAcs8h[:, 0:256]),
                      reads=("ytmp", "dAcs8h"), writes=("ytmp",))
                tk.op("dve", lambda e: e.tensor_copy(
                    dAcs8m[:].rearrange("p (r i) -> p r i", r=2), dup(dAcs8)),
                    reads=("ytmp",), writes=("dAcs8m",))
                tk.op("dve", lambda e: e.tensor_sub(dAcs8, dAcs8, dAcs8m[:, 0:256]),
                      reads=("ytmp", "dAcs8m"), writes=("ytmp",))
                tk.op("dve", lambda e: e.tensor_copy(
                    dAcs8l[:].rearrange("p (r i) -> p r i", r=2), dup(dAcs8)),
                    reads=("ytmp",), writes=("dAcs8l",))
                tk.op("dve", lambda e: e.tensor_copy(sbf_g[:], S[:, gsl]),
                      reads=("S",), writes=("sbf_g",))
                for jc in range(2):
                    mm(5, b_fm[:, g * 512 + cc * 256 + jc * 128:
                               g * 512 + cc * 256 + (jc + 1) * 128],
                       c_fm[:, g * 512 + csl.start:g * 512 + csl.stop],
                       start=True, stop=True, rkeys=("b_fm", "c_fm"),
                       out_sl=pb[5][0:128, 0:256])
                    evac(cbT[:, jc * 256:(jc + 1) * 256], pb[5][0:128, 0:256],
                         ("pb5",), ("cbT",))
                for jc in range(2):
                    tp(5, b_fm[:, g * 512 + cc * 256 + jc * 128:
                               g * 512 + cc * 256 + (jc + 1) * 128],
                       c_idb[:], ("b_fm",), pbb[5][:, 0:128])
                    evac(bT[:, jc * 128:(jc + 1) * 128], pbb[5][:, 0:128],
                         ("pb5",), ("bT",))
                EBUF = [(expb0[:], "expb0"), (expb1[:], "expb1"),
                        (big2b[:, 1024:1280], "big2"), (big2b[:, 1280:1536], "big2")]
                MBUF = [(mT0[:], "mT0"), (mT1[:], "mT1"),
                        (big2b[:, 1536:1792], "big2"), (big2b[:, 1792:2048], "big2")]

                def emit_T1(hh):
                    t1bank = 0 if hh % 2 == 0 else 1
                    for i3, split in enumerate((dAcs8h, dAcs8m, dAcs8l)):
                        mm(t1bank, c_sel8[:, hh * 128:(hh + 1) * 128], split[:],
                           start=(i3 == 0), stop=False,
                           rkeys=("c_sel8", split.name),
                           out_sl=pb[t1bank][0:128, 0:512])
                    mm(t1bank, c_idb[:], c_mask[:],
                       start=False, stop=True, rkeys=("c_idb", "c_mask"),
                       out_sl=pb[t1bank][0:128, 0:512])

                emit_T1(0)
                for rc in range(4):
                    ydbank = 2 if rc % 2 == 0 else 3
                    for sub in range(2):
                        hh = 2 * rc + sub
                        h = g * 8 + hh
                        t1bank = 0 if hh % 2 == 0 else 1
                        if hh + 1 < 8:
                            emit_T1(hh + 1)
                        par = hh % 2
                        for jc in range(2):
                            eb, ebk = EBUF[2 * par + jc]
                            mt, mtk = MBUF[2 * par + jc]
                            tk.op("act", lambda e, x_=eb, tb=t1bank, j_=jc, h_=h:
                                  e.activation(x_, pb[tb][0:128, j_ * 256:(j_ + 1) * 256],
                                               AF.Exp,
                                               bias=colpT[:, j_ * 64 + h_:j_ * 64 + h_ + 1]),
                                  reads=(f"pb{t1bank}", "colpT"), writes=(ebk,))
                            tk.op("dve", lambda e, x_=eb, m_=mt, j_=jc:
                                  e.tensor_mul(m_, x_,
                                               cbT[:, j_ * 256:(j_ + 1) * 256]),
                                  reads=(ebk, "cbT"), writes=(mtk,))
                            mm(ydbank, fmbuf[:, jc * 4096 + h * 64:jc * 4096 + (h + 1) * 64],
                               mt, start=(jc == 0), stop=(jc == 1),
                               rkeys=("fmbuf", mtk),
                               out_sl=pb[ydbank][sub * 64:(sub + 1) * 64, 0:256])
                    yobank = 4 if rc % 2 == 0 else 6
                    mm(yobank, sbf_g[:, rc * 128:(rc + 1) * 128],
                       c_fm[:, g * 512 + csl.start:g * 512 + csl.stop],
                       start=True, stop=True, rkeys=("sbf_g", "c_fm"),
                       out_sl=pb[yobank][0:128, 0:256])
                    mm(7, c_selpair[:, rc * 128:(rc + 1) * 128], e8g[:],
                       start=True, stop=True, rkeys=("c_selpair", "e8g"),
                       out_sl=pb[7][0:128, 0:256])
                    tk.op("dve", lambda e: e.tensor_copy(eb_s[:], pb[7][0:128, 0:256]),
                          reads=("pb7",), writes=("eb_s",))
                    tk.op("dve", lambda e, yb=yobank: e.tensor_mul(
                        ytmp[:], pb[yb][0:128, 0:256], eb_s[:]),
                        reads=(f"pb{yobank}", "eb_s"), writes=("ytmp",))
                    tk.op("dve", lambda e, yd=ydbank: e.tensor_add(
                        ytmp[:], ytmp[:], pb[yd][0:128, 0:256]),
                        reads=("ytmp", f"pb{ydbank}"), writes=("ytmp",))
                    ych = g * 4 + rc
                    ydst = big3b[:, ych * 256:(ych + 1) * 256]
                    tk.op("dve", lambda e, d_=ydst: e.scalar_tensor_tensor(
                        d_, big1b[:, ych * 512 + csl.start:ych * 512 + csl.stop],
                        c_dexp[:, g * 4 + rc:g * 4 + rc + 1], ytmp[:],
                        ALU.mult, ALU.add),
                        reads=("big1", "c_dexp", "ytmp"), writes=("big3",))
                for jc in range(2):
                    src3 = fmbuf[:, jc * 4096 + gsl.start:jc * 4096 + gsl.stop]
                    tk.op("pool", lambda e, s_=src3, j_=jc: e.tensor_tensor(
                        wx[:, j_ * 512:(j_ + 1) * 512].rearrange("p (h q) -> p h q", q=64),
                        s_.rearrange("p (h q) -> p h q", q=64),
                        wT[:, j_ * 64 + g * 8:j_ * 64 + (g + 1) * 8].unsqueeze(2)
                        .broadcast_to([128, 8, 64]),
                        ALU.mult),
                        reads=("fmbuf", "wT"), writes=("wx",))
                for jc in range(2):
                    mm(5, bT[:, jc * 128:(jc + 1) * 128], wx[:, jc * 512:(jc + 1) * 512],
                       start=(jc == 0), stop=(jc == 1), rkeys=("bT", "wx"))
                tk.op("pool", lambda e: e.tensor_tensor(
                    S[:, gsl].rearrange("p (h q) -> p h q", q=64),
                    S[:, gsl].rearrange("p (h q) -> p h q", q=64),
                    decayexp[:, g * 8:(g + 1) * 8].unsqueeze(2).broadcast_to([128, 8, 64]),
                    ALU.mult),
                    reads=("S", "decayexp"), writes=("S",))
                tk.op("dve", lambda e: e.tensor_add(S[:, gsl], S[:, gsl], pb[5][:]),
                      reads=("S", "pb5"), writes=("S",))

            if t == 0 and cc == 0:
                tap("y0", big3b[:, 0:8192], (128, 8192), BF16)

            def zload(cg, buf):
                tk.dma("qact", buf[:].rearrange("p (o i) -> p o i", i=256),
                       zscr[:].rearrange("(o p) t -> p o t", p=128)[
                           :, cg * 2:(cg + 1) * 2, csl],
                       reads=("zscr",), writes=(buf.name,),
                       ch="cza" if buf is zg4a else "czb")

            zload(0, zg4a)
            zload(1, zg4b)
            for c in range(32):
                ys = ysq0 if c % 2 == 0 else ysq1
                tk.op("act", lambda e, y_=ys, c_=c: e.activation(
                    y_[:], big3b[:, c_ * 256:(c_ + 1) * 256], AF.Square),
                    reads=("big3",), writes=(ys.name,))
                mm(0, c_onescolb[:], ys[:], start=(c == 0), stop=(c == 31),
                   rkeys=("c_onescolb", ys.name), out_sl=pb[0][0:1, 0:256])
            tk.op("dve", lambda e: e.tensor_scalar_mul(varrow[:], pb[0][0:1, 0:256],
                                                       1.0 / 4096),
                  reads=("pb0",), writes=("varrow",))
            tk.op("dve", lambda e: e.tensor_scalar(varrow[:], varrow[:],
                                                   c_eps[0:1, :], None, ALU.add),
                  reads=("varrow", "c_eps"), writes=("varrow",))
            tk.op("act", lambda e: e.activation(varrow[:], varrow[:], AF.Ln),
                  reads=("varrow",), writes=("varrow",))
            tk.op("act", lambda e: e.activation(rstdb[:], varrow[:], AF.Exp, scale=-0.5),
                  reads=("varrow",), writes=("rstdb",))
            tk.op("pe", lambda e: e.matmul(pb[1][0:128, 0:256], c_ones1b[:], rstdb[:],
                                           start=True, stop=True),
                  reads=("c_ones1b", "rstdb"), writes=("pb1",))
            tk.op("dve", lambda e: e.tensor_copy(rb_s[:], pb[1][0:128, 0:256]),
                  reads=("pb1",), writes=("rb_s",))
            for c in range(32):
                cg = c // 2
                zbuf = zg4a if cg % 2 == 0 else zg4b
                ys = ysq0 if c % 2 == 0 else ysq1
                tk.op("pool", lambda e, c_=c: e.tensor_mul(
                    ytmp[:], big3b[:, c_ * 256:(c_ + 1) * 256], rb_s[:]),
                    reads=("big3", "rb_s"), writes=("ytmp",))
                tk.op("act", lambda e, y_=ys, z_=zbuf, c_=c: e.activation(
                    y_[:], z_[:, (c_ % 2) * 256:(c_ % 2 + 1) * 256], AF.Silu),
                    reads=(zbuf.name,), writes=(ys.name,))
                tk.op("dve", lambda e, y_=ys, c_=c: e.scalar_tensor_tensor(
                    big3b[:, 8192 + c_ * 256:8192 + (c_ + 1) * 256], ytmp[:],
                    c_normw[:, c_:c_ + 1], y_[:], ALU.mult, ALU.mult),
                    reads=("ytmp", "c_normw", ys.name, "big3"), writes=(f"yn{c}",))
                if c % 2 == 1 and cg + 2 < 16:
                    zload(cg + 2, zbuf)
            if t == 0 and cc == 0:
                tap("yn0", big3b[:, 8192:16384], (128, 8192), BF16)
            obuf = {0: None, 1: None}

            def stream_out(oc):
                buf = wbufA if oc % 2 == 0 else wbufB
                tk.dma("qsp", buf[:], bw_out[oc * 128:(oc + 1) * 128, :],
                       reads=("bw_out",), writes=(buf.name,),
                       ch="cwa" if oc % 2 == 0 else "cwb")
                return buf

            obuf[0] = stream_out(0)
            obuf[1] = stream_out(1)
            for oc in range(16):
                buf = obuf[oc % 2]
                bank = 4 if oc % 2 == 0 else 6
                for ic in range(32):
                    mm(bank, buf[:, ic * 128:(ic + 1) * 128],
                       big3b[:, 8192 + ic * 256:8192 + (ic + 1) * 256],
                       start=(ic == 0), stop=(ic == 31),
                       rkeys=(buf.name, f"yn{ic}"), out_sl=pb[bank][0:128, 0:256])
                if oc + 2 < 16:
                    obuf[oc % 2] = stream_out(oc + 2)
                osb = os0 if oc % 2 == 0 else os1
                evac(osb[:], pb[bank][0:128, 0:256], (f"pb{bank}",), (osb.name,))
                slot = oc % 8
                for bb in range(2):
                    tb = 5 if bb == 0 else 7
                    tp(tb, osb[:, bb * 128:(bb + 1) * 128], c_idb[:],
                       (osb.name,), pbb[tb][:, slot * 128:(slot + 1) * 128])
                if oc % 8 == 7:
                    oc0 = oc - 7
                    for bb in range(2):
                        tb = 5 if bb == 0 else 7
                        b = 2 * cc + bb
                        dst = x_tm[:, b * 2048 + oc0 * 128:b * 2048 + (oc0 + 8) * 128]
                        tk.op("dve", lambda e, d_=dst, tb_=tb: e.tensor_add(
                            d_, d_, pbb[tb_][:, 0:1024]),
                            reads=("x_tm", f"pb{tb}"), writes=("x_tm",))
        if t == 0:
            tap("x1t0", x_tm[:], (128, 8192), F32)

        tk.alias("big3", "yn0", "yn1", "yn2", "yn3", "yn4", "yn5", "yn6", "yn7", "yn8", "yn9", "yn10", "yn11", "yn12", "yn13", "yn14", "yn15", "yn16", "yn17", "yn18", "yn19", "yn20", "yn21", "yn22", "yn23", "yn24", "yn25", "yn26", "yn27", "yn28", "yn29", "yn30", "yn31")
        for b in range(4):
            layernorm_block(b, sqv, "big3")
            transpose_to_fm(None, b, tbank=6)
        for b in range(4):
            for dc in range(16):
                mm(0, fmbuf[:, dc * 512 + b * 128:dc * 512 + (b + 1) * 128],
                   c_gateb[:, dc * 8:(dc + 1) * 8],
                   start=(dc == 0), stop=(dc == 15), rkeys=("fmbuf", "c_gateb"),
                   out_sl=pb[0][0:128, 0:8])
            tk.op("dve", lambda e: e.tensor_copy(glog[:], pb[0][0:128, 0:8]),
                  reads=("pb0",), writes=("glog",))
            tk.op("dve", lambda e: e.reduce_max(gm1[:], glog[:], AX.X),
                  reads=("glog",), writes=("gm1",))
            tk.op("dve", lambda e: e.tensor_scalar_mul(gm1[:], gm1[:], -1.0),
                  reads=("gm1",), writes=("gm1",))
            tk.op("act", lambda e: e.activation(gsc[:], glog[:], AF.Exp, bias=gm1[:]),
                  reads=("glog", "gm1"), writes=("gsc",))
            tk.op("dve", lambda e: e.reduce_sum(gm1[:], gsc[:], AX.X),
                  reads=("gsc",), writes=("gm1",))
            tk.op("dve", lambda e: e.reciprocal(gm1[:], gm1[:]),
                  reads=("gm1",), writes=("gm1",))
            tk.op("dve", lambda e: e.tensor_scalar(gsc[:], gsc[:], gm1[:], None,
                                                   ALU.mult),
                  reads=("gsc", "gm1"), writes=("gsc",))
            tk.op("dve", lambda e: e.reduce_max(gm1[:], gsc[:], AX.X),
                  reads=("gsc",), writes=("gm1",))
            tk.op("dve", lambda e: e.tensor_scalar(gtmp[:], gsc[:], gm1[:], None,
                                                   ALU.is_ge),
                  reads=("gsc", "gm1"), writes=("gtmp",))
            tk.op("dve", lambda e: e.scalar_tensor_tensor(gtmp[:], gtmp[:], -1e30,
                                                          gsc[:], ALU.mult, ALU.add),
                  reads=("gtmp", "gsc"), writes=("gtmp",))
            tk.op("dve", lambda e: e.reduce_max(gm1[:], gtmp[:], AX.X),
                  reads=("gtmp",), writes=("gm1",))
            tk.op("dve", lambda e: e.tensor_scalar(gtmp[:], gsc[:], gm1[:], None,
                                                   ALU.is_ge),
                  reads=("gsc", "gm1"), writes=("gtmp",))
            tk.op("dve", lambda e: e.tensor_mul(gsc[:], gsc[:], gtmp[:]),
                  reads=("gsc", "gtmp"), writes=("gsc",))
            tp(1, gsc[:, 0:8], c_id[:], ("gsc",), pb[1][0:8, 0:128])
            tk.op("dve", lambda e, b_=b: e.tensor_copy(
                comb_fm[:, b_ * 128:(b_ + 1) * 128], pb[1][0:8, 0:128]),
                reads=("pb1",), writes=("comb_fm",))
        if t == 0:
            tap("comb0", comb_fm[:], (8, 512), BF16)

        g_s = big2b[:, 0:5632]
        wdA = big3b[:, 0:5632]
        wdB = big3b[:, 5632:11264]
        tk.alias("wdA", "big3", "yn0", "yn1", "yn2", "yn3", "yn4", "yn5", "yn6", "yn7", "yn8", "yn9", "yn10", "yn11", "yn12", "yn13", "yn14", "yn15", "yn16", "yn17", "yn18", "yn19", "yn20", "yn21", "yn22", "yn23", "yn24", "yn25", "yn26", "yn27", "yn28", "yn29", "yn30", "yn31")
        tk.alias("wdB", "big3", "yn0", "yn1", "yn2", "yn3", "yn4", "yn5", "yn6", "yn7", "yn8", "yn9", "yn10", "yn11", "yn12", "yn13", "yn14", "yn15", "yn16", "yn17", "yn18", "yn19", "yn20", "yn21", "yn22", "yn23", "yn24", "yn25", "yn26", "yn27", "yn28", "yn29", "yn30", "yn31")

        def up_stage(hc, buf, gb, ub, comb_bank=None):
            for dc in range(16):
                mm(gb, buf[:, dc * 128:(dc + 1) * 128],
                   fmbuf[:, dc * 512:(dc + 1) * 512], start=(dc == 0),
                   stop=(dc == 15), rkeys=(buf.name, "fmbuf"))
            for dc in range(16):
                mm(ub, buf[:, 2048 + dc * 128:2048 + (dc + 1) * 128],
                   fmbuf[:, dc * 512:(dc + 1) * 512], start=(dc == 0),
                   stop=(dc == 15), rkeys=(buf.name, "fmbuf"))
            sl = hc % 11
            zz = zstage0 if hc % 2 == 0 else zstage1
            tk.op("act", lambda e, z_=zz, g_=gb: e.activation(z_[:], pb[g_][:], AF.Silu),
                  reads=(f"pb{gb}",), writes=(zz.name,))
            tk.op("dve", lambda e, z_=zz, u_=ub, s_=sl: e.tensor_tensor(
                g_s[:, s_ * 512:(s_ + 1) * 512], z_[:], pb[u_][:], ALU.mult),
                reads=(zz.name, f"pb{ub}"), writes=("big2",))
            if comb_bank is not None:
                tk.op("dve", lambda e, s_=sl: e.tensor_tensor(
                    g_s[:, s_ * 512:(s_ + 1) * 512],
                    g_s[:, s_ * 512:(s_ + 1) * 512],
                    pb[comb_bank][:], ALU.mult),
                    reads=("big2", f"pb{comb_bank}"), writes=("big2",))

        def sgu_dma(hc, buf):
            tk.dma("qsp", buf[:], bw_sgu[hc * 128:(hc + 1) * 128, :],
                   reads=("bw_sgu",), writes=(buf.name,),
                   ch="cwa" if buf is wbufA else "cwb")

        def sd_dma(dg, buf, bkey):
            tk.dma("qsp", buf, bw_sd[dg * 128:(dg + 1) * 128, :],
                   reads=("bw_sd",), writes=(bkey,),
                   ch="cda" if bkey == "wdA" else "cdb")

        bufs2 = [wbufA, wbufB]
        for sp_pass in range(2):
            h0 = sp_pass * 11
            sgu_dma(h0 + 0, bufs2[0])
            sgu_dma(h0 + 1, bufs2[1])
            for hl in range(11):
                hc = h0 + hl
                buf = bufs2[hl % 2]
                gb = 2 if hl % 2 == 0 else 4
                ub = 3 if hl % 2 == 0 else 6
                up_stage(hc, buf, gb, ub)
                if hl + 2 < 11:
                    sgu_dma(h0 + hl + 2, bufs2[hl % 2])
            sd_dma(0, wdA, "wdA")
            sd_dma(1, wdB, "wdB")
            for dg in range(8):
                buf, bkey = (wdA, "wdA") if dg % 2 == 0 else (wdB, "wdB")
                for sub in range(2):
                    dc = dg * 2 + sub
                    bank = 0 if dc % 2 == 0 else 1
                    for hl in range(11):
                        hc = h0 + hl
                        mm(bank, buf[:, sub * 2816 + hc * 128:sub * 2816 + (hc + 1) * 128],
                           g_s[:, hl * 512:(hl + 1) * 512], start=(hl == 0),
                           stop=(hl == 10), rkeys=(bkey, "big2"))
                    if sp_pass == 0:
                        tk.op("dve", lambda e, dc_=dc, b_=bank: e.tensor_copy(
                            big1[:, dc_ * 512:(dc_ + 1) * 512], pb[b_][:]),
                            reads=(f"pb{bank}",), writes=("big1",))
                    else:
                        tk.op("dve", lambda e, dc_=dc, b_=bank: e.tensor_add(
                            big1[:, dc_ * 512:(dc_ + 1) * 512],
                            big1[:, dc_ * 512:(dc_ + 1) * 512], pb[b_][:]),
                            reads=("big1", f"pb{bank}"), writes=("big1",))
                if dg + 2 < 8:
                    sd_dma(dg + 2, buf, bkey)

        def gu_dma(ex_, hc, buf):
            row0 = (ex_ * 11 + hc) * 128
            tk.dma("qsp", buf[:], bw_gu[row0:row0 + 128, :],
                   reads=("bw_gu",), writes=(buf.name,),
                   ch="cwa" if buf is wbufA else "cwb")

        def d_dma(ex_, dg, buf, bkey):
            row0 = (ex_ * 4 + dg) * 128
            tk.dma("qsp", buf, bw_d[row0:row0 + 128, :],
                   reads=("bw_d",), writes=(bkey,),
                   ch="cda" if bkey == "wdA" else "cdb")

        for ex in range(EXP):
            tk.op("pe", lambda e, x_=ex: e.matmul(
                pb[7][0:128, 0:512], c_sel8[:, x_ * 128:(x_ + 1) * 128], comb_fm[:],
                start=True, stop=True),
                reads=("c_sel8", "comb_fm"), writes=("pb7",))
            gu_dma(ex, 0, wbufA)
            gu_dma(ex, 1, wbufB)
            for hc in range(11):
                buf = bufs2[hc % 2]
                gb = 2 if hc % 2 == 0 else 4
                ub = 3 if hc % 2 == 0 else 6
                up_stage(hc, buf, gb, ub, comb_bank=7)
                if hc + 2 < 11:
                    gu_dma(ex, hc + 2, bufs2[hc % 2])
            d_dma(ex, 0, wdA, "wdA")
            d_dma(ex, 1, wdB, "wdB")
            for dg in range(4):
                buf, bkey = (wdA, "wdA") if dg % 2 == 0 else (wdB, "wdB")
                for sub in range(4):
                    dc = dg * 4 + sub
                    bank = 0 if dc % 2 == 0 else 1
                    for hc in range(11):
                        mm(bank, buf[:, sub * 1408 + hc * 128:sub * 1408 + (hc + 1) * 128],
                           g_s[:, hc * 512:(hc + 1) * 512], start=(hc == 0),
                           stop=(hc == 10), rkeys=(bkey, "big2"))
                    tk.op("dve", lambda e, dc_=dc, b_=bank: e.tensor_add(
                        big1[:, dc_ * 512:(dc_ + 1) * 512],
                        big1[:, dc_ * 512:(dc_ + 1) * 512], pb[b_][:]),
                        reads=("big1", f"pb{bank}"), writes=("big1",))
                if dg + 2 < 4:
                    d_dma(ex, dg + 2, buf, bkey)

        for b in range(4):
            for half in range(4):
                for k in range(4):
                    dc = half * 4 + k
                    tp(5, big1[:, dc * 512 + b * 128:dc * 512 + (b + 1) * 128],
                       c_id[:], ("big1",), pb[5][:, k * 128:(k + 1) * 128])
                dst = x_tm[:, b * 2048 + half * 512:b * 2048 + (half + 1) * 512]
                tk.op("dve", lambda e, d_=dst: e.tensor_add(d_, d_, pb[5][:]),
                      reads=("x_tm", "pb5"), writes=("x_tm",))
            tk.dma("qact", out[T0 + b * 128:T0 + (b + 1) * 128, :],
                   x_tm[:, b * 2048:(b + 1) * 2048],
                   reads=("x_tm",), writes=("out",), ch="cout")

    tk.finish()
    nc.finalize()
    return nc



_CACHE = {}


BUILD_TILES = N_TILES
BUILD_STAGE = "full"


def kernel(**inputs):
    const, xs = prep_inputs(inputs)
    nc = build_program(BUILD_TILES, tuple(DEBUG_TAPS), BUILD_STAGE)
    in_maps = []
    for c in range(8):
        m = dict(const)
        m["xin"] = xs[c]
        in_maps.append(m)
    res = run_bass_kernel_spmd(nc, in_maps, core_ids=list(range(8)))
    outs = np.stack([res.results[c]["out"] for c in range(8)], axis=0)
    if DEBUG_TAPS:
        kernel.taps = [{k.replace("tap_", ""): v for k, v in r.items()
                        if k.startswith("tap_")} for r in res.results]
    return outs


# version 12
# speedup vs baseline: 1.0234x; 1.0234x over previous
import numpy as np
import ml_dtypes
import concourse.bass as bass
import concourse.mybir as mybir
from concourse.bass_utils import run_bass_kernel_spmd

F32 = mybir.dt.float32
BF16 = mybir.dt.bfloat16
AF = mybir.ActivationFunctionType
ALU = mybir.AluOpType
AX = mybir.AxisListType
BF = ml_dtypes.bfloat16

HID = 2048
INTER = 4096
CONVD = 6144
PROJ = 10304
NH, PDIM, NG, NST, CS = 64, 64, 8, 128, 256
LSEQ = 4096
TILE = 512
EXP, HEXP, HSH = 8, 1408, 2816
EPS = 1e-5

N_TILES = LSEQ // TILE
DEBUG_TAPS = []

import bisect


class TK:

    def __init__(self, nc):
        self.nc = nc
        self.E = {"pe": nc.tensor, "dve": nc.vector, "act": nc.scalar,
                  "pool": nc.gpsimd, "sp": nc.sync}
        self.QENG = {"qsp": "sp", "qact": "act", "qpool": "pool"}
        self.sem = {k: nc.alloc_semaphore("sem_" + k)
                    for k in ["pe", "dve", "act", "pool"]}
        self.n = {k: 0 for k in self.E}
        self.insts = {k: [] for k in self.E}
        self.isdma = {k: [] for k in self.E}
        self.incs = {k: [] for k in ["pe", "dve", "act", "pool"]}
        self.qcnt = {}
        self.lastw = {}
        self.readers = {}
        self.waited = {k: {} for k in self.E}

    def chan(self, name):
        if name not in self.qcnt:
            self.qcnt[name] = 0
            self.sem[name] = self.nc.alloc_semaphore("sem_" + name)
        return name

    def _wait(self, stream, semk, val):
        if val <= self.waited[stream].get(semk, 0):
            return
        self.E[stream].wait_ge(self.sem[semk], val)
        self.waited[stream][semk] = val

    def _need(self, stream, src):
        if src[0] == "q":
            _, ch, cnt = src
            self._wait(stream, ch, 16 * cnt)
        else:
            _, eng, idx = src
            if eng == stream and eng == "pe":
                return
            arr = self.incs[eng]
            i = bisect.bisect_left(arr, (idx, -1))
            if i < len(arr):
                pos, cum = arr[i]
            else:
                assert not self.isdma[eng][idx], "inc attach on dma instr"
                h = self.insts[eng][idx]
                cum = (arr[-1][1] + 1) if arr else 1
                h.then_inc(self.sem[eng], 1)
                arr.append((idx, cum))
            self._wait(stream, eng, cum)

    def _pre(self, stream, reads, writes):
        for k in reads:
            lw = self.lastw.get(k)
            if lw is not None:
                self._need(stream, lw)
        for k in writes:
            lw = self.lastw.get(k)
            if lw is not None:
                self._need(stream, lw)
            for src in self.readers.get(k, {}).values():
                self._need(stream, src)

    def _post(self, me, srcname, reads, writes):
        for k in reads:
            self.readers.setdefault(k, {})[srcname] = me
        for k in writes:
            self.lastw[k] = me
            self.readers[k] = {}

    def op(self, stream, emit, reads=(), writes=()):
        self._pre(stream, reads, writes)
        h = emit(self.E[stream])
        idx = self.n[stream]
        self.insts[stream].append(h)
        self.isdma[stream].append(False)
        self.n[stream] += 1
        self._post(("e", stream, idx), stream, reads, writes)
        return h

    def dma(self, q, out_ap, in_ap, reads=(), writes=(), ch=None, **kw):
        stream = self.QENG[q]
        assert ch is not None
        self.chan(ch)
        self._pre(stream, reads, writes)
        h = self.E[stream].dma_start(out_ap, in_ap, **kw)
        self.insts[stream].append(h)
        self.isdma[stream].append(True)
        self.n[stream] += 1
        self.qcnt[ch] += 1
        h.then_inc(self.sem[ch], 16)
        self._post(("q", ch, self.qcnt[ch]), ch, reads, writes)
        return h

    def alias(self, dst, *srcs):
        best = None
        rd = {}
        for s in srcs:
            lw = self.lastw.get(s)
            if lw is not None:
                self.lastw.setdefault(dst, None)
                rd[("w",) + lw] = lw
            for name, src in self.readers.get(s, {}).items():
                rd[("r", name, s)] = src
        if rd:
            d = self.readers.setdefault(dst, {})
            for kk, src in rd.items():
                d[kk] = src

    def finish(self):
        for ch in list(self.qcnt):
            self._wait("sp", ch, 16 * self.qcnt[ch])
        for eng in ["pe", "dve", "act", "pool"]:
            i = self.n[eng] - 1
            while i >= 0 and self.isdma[eng][i]:
                i -= 1
            if i >= 0:
                self._need("sp", ("e", eng, i))




def _tile2d(w, rows, cols, rtile=128, ctile=128):
    rt, ct = rows // rtile, cols // ctile
    g = w.reshape(rt, rtile, ct, ctile).transpose(2, 1, 0, 3)
    return np.ascontiguousarray(g.reshape(ct * rtile, rt * ctile))


def prep_inputs(ins):
    d = {}
    f32 = np.float32
    W_inT = np.ascontiguousarray(ins["L0_attn_in_proj_w"].T)
    d["w_in"] = _tile2d(W_inT[:, :10240], 2048, 10240)
    d["w_indt"] = _tile2d(W_inT[:, 10240:], 2048, 64, ctile=64)
    W_outT = np.ascontiguousarray(ins["L0_attn_out_proj_w"].T)
    d["w_out"] = _tile2d(W_outT, 4096, 2048)

    wg, wu, wd = ins["L0_ffn_w_gate"], ins["L0_ffn_w_up"], ins["L0_ffn_w_down"]
    gu = []
    for e in range(EXP):
        tg = _tile2d(wg[e], 2048, 1408)
        tu = _tile2d(wu[e], 2048, 1408)
        gu.append(np.concatenate(
            [tg.reshape(11, 128, 2048), tu.reshape(11, 128, 2048)], axis=2))
    d["w_gu"] = np.ascontiguousarray(
        np.stack(gu).reshape(EXP * 11 * 128, 4096))
    wdt = []
    for e in range(EXP):
        t = _tile2d(wd[e], 1408, 2048)
        wdt.append(t.reshape(4, 4, 128, 1408).transpose(0, 2, 1, 3).reshape(4 * 128, 4 * 1408))
    d["w_d"] = np.ascontiguousarray(np.stack(wdt).reshape(EXP * 4 * 128, 4 * 1408))

    tsg = _tile2d(ins["L0_ffn_sw_gate"], 2048, 2816)
    tsu = _tile2d(ins["L0_ffn_sw_up"], 2048, 2816)
    d["w_sgu"] = np.ascontiguousarray(np.concatenate(
        [tsg.reshape(22, 128, 2048), tsu.reshape(22, 128, 2048)], axis=2
    ).reshape(22 * 128, 4096))
    tsd = _tile2d(ins["L0_ffn_sw_down"], 2816, 2048)
    d["w_sd"] = np.ascontiguousarray(
        tsd.reshape(8, 2, 128, 2816).transpose(0, 2, 1, 3).reshape(8 * 128, 2 * 2816))

    d["w_gate"] = np.ascontiguousarray(
        ins["L0_ffn_gate_w"].reshape(16, 128, 8).transpose(1, 0, 2).reshape(128, 128))

    d["conv_w_t"] = np.ascontiguousarray(
        ins["L0_attn_conv_w"].reshape(48, 128, 4).transpose(1, 0, 2).reshape(128, 192))
    d["conv_b_t"] = np.ascontiguousarray(ins["L0_attn_conv_b"].reshape(48, 128).T)
    d["norm_w_t"] = np.ascontiguousarray(ins["L0_attn_norm_w"].reshape(32, 128).T)
    d["dt_bias_c"] = ins["L0_attn_dt_bias"].reshape(64, 1).astype(f32)
    d["a_log_c"] = ins["L0_attn_A_log"].reshape(64, 1).astype(f32)
    pp = np.arange(128)[:, None, None] // 64
    gg = np.arange(8)[None, :, None]
    rr = np.arange(4)[None, None, :]
    idx = gg * 8 + 2 * rr + pp
    d["d_exp"] = np.ascontiguousarray(
        ins["L0_attn_D"][idx.reshape(128, 32)]).astype(f32)

    d["id128"] = np.eye(128, dtype=f32)
    d["id128b"] = np.eye(128, dtype=f32).astype(BF)
    d["ones1b"] = np.ones((1, 128), dtype=f32).astype(BF)
    d["ones1f"] = np.ones((1, 128), dtype=f32)
    d["onescolb"] = np.ones((128, 1), dtype=f32).astype(BF)
    sel8 = np.zeros((8, 8 * 128), dtype=f32)
    for h in range(8):
        sel8[h, h * 128:(h + 1) * 128] = 1.0
    d["sel8"] = sel8.astype(BF)
    selp = np.zeros((8, 4 * 128), dtype=f32)
    for rc in range(4):
        selp[2 * rc, rc * 128:rc * 128 + 64] = 1.0
        selp[2 * rc + 1, rc * 128 + 64:rc * 128 + 128] = 1.0
    d["selpair"] = selp.astype(BF)
    mask = np.zeros((128, 512), dtype=f32)
    for jc in range(2):
        for p in range(128):
            for i in range(256):
                if i < jc * 128 + p:
                    mask[p, jc * 256 + i] = -1e30
    d["maskneg"] = mask.astype(BF)
    d["zcol"] = np.zeros((128, 1), dtype=f32)
    d["epscol"] = np.full((128, 1), EPS, dtype=f32)

    xs = [np.ascontiguousarray(ins["L0_attn_hidden_states"][c]) for c in range(8)]
    return d, xs




def build_program(n_tiles=N_TILES, taps=(), stage="full"):
    nc = bass.Bass("TRN2", target_bir_lowering=False)
    tk = TK(nc)
    for _c in ["cc", "ci", "cid", "co", "cgu", "cd", "csg", "csd", "cx",
               "cwa", "cwb", "cda", "cdb", "cza", "czb", "czs", "cout", "ctap"]:
        tk.chan(_c)
    for _s in tk.sem.values():
        nc.gpsimd.sem_clear(_s)
    nc.all_engine_barrier()

    def din(name, shape, dt=F32):
        return nc.dram_tensor(name, list(shape), dt, kind="ExternalInput").ap()

    xin = din("xin", (LSEQ, HID))
    w_in = din("w_in", (80 * 128, 2048))
    w_indt = din("w_indt", (128, 1024))
    w_out = din("w_out", (16 * 128, 4096))
    w_gu = din("w_gu", (EXP * 11 * 128, 4096))
    w_d = din("w_d", (EXP * 4 * 128, 4 * 1408))
    w_sgu = din("w_sgu", (22 * 128, 4096))
    w_sd = din("w_sd", (8 * 128, 2 * 2816))
    w_gate = din("w_gate", (128, 128))
    conv_w_t = din("conv_w_t", (128, 192))
    conv_b_t = din("conv_b_t", (128, 48))
    norm_w_t = din("norm_w_t", (128, 32))
    dt_bias_c = din("dt_bias_c", (64, 1))
    a_log_c = din("a_log_c", (64, 1))
    d_exp = din("d_exp", (128, 32))
    id128 = din("id128", (128, 128))
    id128b = din("id128b", (128, 128), BF16)
    ones1b = din("ones1b", (1, 128), BF16)
    ones1f = din("ones1f", (1, 128))
    onescolb = din("onescolb", (128, 1), BF16)
    sel8 = din("sel8", (8, 1024), BF16)
    selpair = din("selpair", (8, 512), BF16)
    maskneg = din("maskneg", (128, 512), BF16)
    zcol = din("zcol", (128, 1))
    epscol = din("epscol", (128, 1))

    out = nc.dram_tensor("out", [LSEQ, HID], F32, kind="ExternalOutput").ap()

    def scr(name, shape):
        return nc.dram_tensor(name, list(shape), BF16, kind="Internal").ap()

    bw_in = scr("bw_in", (80 * 128, 2048))
    bw_indt = scr("bw_indt", (128, 1024))
    bw_out = scr("bw_out", (16 * 128, 4096))
    bw_gu = scr("bw_gu", (EXP * 11 * 128, 4096))
    bw_d = scr("bw_d", (EXP * 4 * 128, 4 * 1408))
    bw_sgu = scr("bw_sgu", (22 * 128, 4096))
    bw_sd = scr("bw_sd", (8 * 128, 2 * 2816))
    zscr = scr("zscr", (32 * 128, 512))

    def sb(name, shape, dt):
        return nc.alloc_sbuf_tensor(name, list(shape), dt)

    x_tm = sb("x_tm", (128, 4 * 2048), F32)
    fmbuf = sb("fmbuf", (128, 8192), BF16)
    big1 = sb("big1", (128, 8192), F32)
    b_fm = sb("b_fm", (128, 4096), BF16)
    c_fm = sb("c_fm", (128, 4096), BF16)
    big2 = sb("big2", (128, 2816), F32)
    big3 = sb("big3", (128, 8192), F32)
    S = sb("S", (128, 4096), F32)
    sbf_g = sb("sbf_g", (128, 512), BF16)
    wbufA = sb("wbufA", (128, 4096), BF16)
    wbufB = sb("wbufB", (128, 4096), BF16)
    xbcs0 = sb("xbcs0", (128, 516), BF16)
    xbcs1 = sb("xbcs1", (128, 516), BF16)
    convacc0 = sb("convacc0", (128, 512), BF16)
    convacc1 = sb("convacc1", (128, 512), BF16)
    tail = sb("tail", (128, 144), BF16)
    dts = sb("dts", (64, 512), F32)
    dAcs = sb("dAcs", (64, 256), F32)
    dAcsT = sb("dAcsT", (128, 128), F32)
    colpT = sb("colpT", (128, 128), F32)
    wT = sb("wT", (128, 128), F32)
    dAcs8h = sb("dAcs8h", (8, 512), BF16)
    dAcs8m = sb("dAcs8m", (8, 512), BF16)
    dAcs8l = sb("dAcs8l", (8, 512), BF16)
    e8g = sb("e8g", (8, 256), BF16)
    decrow = sb("decrow", (1, 64), F32)
    decayexp = sb("decayexp", (128, 64), F32)
    cbT = sb("cbT", (128, 512), BF16)
    bT = sb("bT", (128, 256), BF16)
    wx = sb("wx", (128, 1024), BF16)
    expb0 = sb("expb0", (128, 256), BF16)
    expb1 = sb("expb1", (128, 256), BF16)
    mT0 = sb("mT0", (128, 256), BF16)
    mT1 = sb("mT1", (128, 256), BF16)
    eb_s = sb("eb_s", (128, 256), BF16)
    ytmp = sb("ytmp", (128, 256), F32)
    ysq0 = sb("ysq0", (128, 256), BF16)
    ysq1 = sb("ysq1", (128, 256), BF16)
    varrow = sb("varrow", (1, 256), F32)
    rstdb = sb("rstdb", (1, 256), BF16)
    rb_s = sb("rb_s", (128, 256), BF16)
    os0 = sb("os0", (128, 256), BF16)
    os1 = sb("os1", (128, 256), BF16)
    colm = sb("colm", (128, 1), F32)
    colv = sb("colv", (128, 1), F32)
    colt = sb("colt", (128, 1), F32)
    zg4a = sb("zg4a", (128, 512), BF16)
    zg4b = sb("zg4b", (128, 512), BF16)
    glog = sb("glog", (128, 8), F32)
    gsc = sb("gsc", (128, 8), F32)
    gtmp = sb("gtmp", (128, 8), F32)
    gm1 = sb("gm1", (128, 1), F32)
    comb_fm = sb("comb_fm", (8, 512), BF16)
    c_convw = sb("c_convw", (128, 192), F32)
    c_convb = sb("c_convb", (128, 48), F32)
    c_normw = sb("c_normw", (128, 32), F32)
    c_dtb = sb("c_dtb", (64, 1), F32)
    c_acol = sb("c_acol", (64, 1), F32)
    c_dexp = sb("c_dexp", (128, 32), F32)
    c_id = sb("c_id", (128, 128), F32)
    c_idb = sb("c_idb", (128, 128), BF16)
    c_ones1b = sb("c_ones1b", (1, 128), BF16)
    c_ones1f = sb("c_ones1f", (1, 128), F32)
    c_onescolb = sb("c_onescolb", (128, 1), BF16)
    c_sel8 = sb("c_sel8", (8, 1024), BF16)
    c_selpair = sb("c_selpair", (8, 512), BF16)
    c_mask = sb("c_mask", (128, 512), BF16)
    c_zcol = sb("c_zcol", (128, 1), F32)
    c_eps = sb("c_eps", (128, 1), F32)
    c_gateb = sb("c_gateb", (128, 128), BF16)

    big1b = big1[:].bitcast(BF16)
    big2b = big2[:].bitcast(BF16)
    big3b = big3[:].bitcast(BF16)

    pb = [nc.alloc_psum_tensor(f"pb{i}", [128, 512], F32) for i in range(8)]
    pbb = [p[:].bitcast(BF16) for p in pb]

    lnstage = big3b[:, 8192:10240]
    zstage0, zstage1 = convacc0, convacc1
    dA = big3[0:64, 2560:2816]
    lndt = big3[0:64, 2816:3072]
    w_hm = ytmp[0:64, 0:256]

    def K(ap_or_name):
        return ap_or_name if isinstance(ap_or_name, str) else ap_or_name.name

    def mm(bank, lhsT, rhs, start, stop, rkeys, out_sl=None):
        o = pb[bank][:] if out_sl is None else out_sl
        tk.op("pe", lambda e: e.matmul(o, lhsT, rhs, start=start, stop=stop),
              reads=tuple(rkeys), writes=(f"pb{bank}",))

    def tp(bank, in_ap, ident, rkeys, out_sl):
        tk.op("pe", lambda e: e.transpose(out_sl, in_ap, ident),
              reads=tuple(rkeys), writes=(f"pb{bank}",))

    taps_out = {}

    def warm(n):
        for _ in range(n):
            tk.op("pe", lambda e: e.matmul(pb[6][0:128, 0:512], c_idb[:],
                                           c_mask[:], start=True, stop=True),
                  reads=(), writes=())

    def tap(name, ap, shape, dt=F32):
        if name not in taps:
            return
        t = nc.dram_tensor("tap_" + name, list(shape), dt, kind="ExternalOutput").ap()
        tk._wait("act", "ctap", 16 * tk.qcnt["ctap"])
        tk.dma("qact", t, ap, reads=(ap.name,), ch="ctap")
        taps_out[name] = True

    for dst, src in [
        (c_convw[:], conv_w_t), (c_convb[:], conv_b_t), (c_normw[:], norm_w_t),
        (c_dtb[:], dt_bias_c), (c_dexp[:], d_exp), (c_id[:], id128),
        (c_idb[:], id128b), (c_ones1b[:], ones1b), (c_ones1f[:], ones1f),
        (c_onescolb[:], onescolb), (c_sel8[:], sel8), (c_selpair[:], selpair),
        (c_mask[:], maskneg), (c_zcol[:], zcol), (c_eps[:], epscol),
    ]:
        tk.dma("qsp", dst, src, writes=(dst.name,), ch="cc")
    tk.dma("qsp", colt[0:64, :], a_log_c, writes=("colt",), ch="cc")
    tk.dma("qsp", big3[:, 0:128], w_gate, writes=("big3",), ch="cc")
    _cq = 16 * tk.qcnt["cc"]
    for _eng in ["pe", "dve", "act", "pool", "sp"]:
        tk._wait(_eng, "cc", _cq)
    tk.op("act", lambda e: e.activation(c_acol[0:64, :], colt[0:64, :], AF.Exp),
          reads=("colt",), writes=("c_acol",))
    tk.op("dve", lambda e: e.tensor_scalar_mul(c_acol[0:64, :], c_acol[0:64, :], -1.0),
          reads=("c_acol",), writes=("c_acol",))
    tk.op("dve", lambda e: e.tensor_copy(c_gateb[:], big3[:, 0:128]),
          reads=("big3",), writes=("c_gateb",))
    tk.op("dve", lambda e: e.memset(S[:], 0.0), writes=("S",))
    tk.op("dve", lambda e: e.memset(tail[:], 0.0), writes=("tail",))

    def cast_all(ch, dst, src, rows, chunk_rows=1024):
        r = 0
        while r < rows:
            rr = min(chunk_rows, rows - r)
            tk.dma("qpool", dst[r:r + rr, :], src[r:r + rr, :],
                   reads=(src.name,), writes=(dst.name,), ch=ch)
            r += rr

    if stage != "ln1_nocast":
        cast_all("ci", bw_in, w_in, 80 * 128, 2048)
        cast_all("cid", bw_indt, w_indt, 128)
        cast_all("co", bw_out, w_out, 16 * 128, 1024)
        cast_all("csg", bw_sgu, w_sgu, 22 * 128, 1024)
        cast_all("cgu", bw_gu, w_gu, EXP * 11 * 128, 1024)
        cast_all("cd", bw_d, w_d, EXP * 4 * 128, 512)
        cast_all("csd", bw_sd, w_sd, 8 * 128, 512)

    evac_rr = [0]

    def evac(dst_ap, src_ap, rkeys, wkeys):
        eng = "dve" if evac_rr[0] % 2 == 0 else "act"
        evac_rr[0] += 1
        if eng == "dve":
            tk.op("dve", lambda e: e.tensor_copy(dst_ap, src_ap),
                  reads=tuple(rkeys), writes=tuple(wkeys))
        else:
            tk.op("act", lambda e: e.copy(dst_ap, src_ap),
                  reads=tuple(rkeys), writes=tuple(wkeys))

    def layernorm_block(b, sq_scr, sq_key):
        xb = x_tm[:, b * 2048:(b + 1) * 2048]
        tk.op("dve", lambda e: e.reduce_sum(colm[:], xb, AX.X),
              reads=("x_tm",), writes=("colm",))
        tk.op("act", lambda e: e.activation(sq_scr, xb, AF.Square, accum_out=colv[:]),
              reads=("x_tm",), writes=(sq_key, "colv"))
        tk.op("dve", lambda e: e.tensor_scalar_mul(colm[:], colm[:], 1.0 / 2048),
              reads=("colm",), writes=("colm",))
        tk.op("dve", lambda e: e.tensor_scalar_mul(colv[:], colv[:], 1.0 / 2048),
              reads=("colv",), writes=("colv",))
        tk.op("dve", lambda e: e.tensor_mul(colt[:], colm[:], colm[:]),
              reads=("colm",), writes=("colt",))
        tk.op("dve", lambda e: e.tensor_sub(colv[:], colv[:], colt[:]),
              reads=("colv", "colt"), writes=("colv",))
        tk.op("dve", lambda e: e.tensor_scalar(colv[:], colv[:], c_eps[:], None,
                                               ALU.add),
              reads=("colv", "c_eps"), writes=("colv",))
        tk.op("act", lambda e: e.activation(colv[:], colv[:], AF.Ln),
              reads=("colv",), writes=("colv",))
        tk.op("act", lambda e: e.activation(colv[:], colv[:], AF.Exp, scale=-0.5),
              reads=("colv",), writes=("colv",))
        tk.op("dve", lambda e: e.tensor_scalar(lnstage, xb, colm[:], colv[:],
                                               ALU.subtract, ALU.mult),
              reads=("x_tm", "colm", "colv"), writes=("big3",))

    def transpose_to_fm(dst_col_of_chunk, b, tbank):
        for half in range(2):
            bk = 2 if (b * 2 + half) % 2 == 0 else 3
            for k in range(8):
                c = half * 8 + k
                tp(bk, lnstage[:, c * 128:(c + 1) * 128], c_idb[:],
                   ("big3",), pbb[bk][:, k * 128:(k + 1) * 128])
            dst = fmbuf[:].rearrange("p (c t) -> p c t", t=512)[
                :, half * 8:half * 8 + 8, b * 128:(b + 1) * 128]
            tk.op("dve", lambda e, d=dst, s=pbb[bk][:, 0:1024]: e.tensor_copy(d, s),
                  reads=(f"pb{bk}",), writes=("fmbuf",))

    sqv = big3[:, 0:2048]

    for t in range(n_tiles):
        T0 = t * TILE
        tk.alias("big3", "wdA", "wdB", "yn0", "yn1", "yn2", "yn3", "yn4", "yn5", "yn6", "yn7", "yn8", "yn9", "yn10", "yn11", "yn12", "yn13", "yn14", "yn15", "yn16", "yn17", "yn18", "yn19", "yn20", "yn21", "yn22", "yn23", "yn24", "yn25", "yn26", "yn27", "yn28", "yn29", "yn30", "yn31")
        for b in range(4):
            tk.dma("qact", x_tm[:, b * 2048:(b + 1) * 2048],
                   xin[T0 + b * 128:T0 + (b + 1) * 128, :],
                   reads=("xin",), writes=("x_tm",), ch="cx")
        if t == 0:
            tap("xin0", x_tm[:], (128, 8192), F32)
        for b in range(4):
            layernorm_block(b, sqv, "big3")
            warm(10)
            if t == 0 and b == 0:
                tap("colm0", colm[:], (128, 1), F32)
                tap("colv0", colv[:], (128, 1), F32)
                tap("lnst0", lnstage, (128, 2048), BF16)
            transpose_to_fm(None, b, tbank=6)
        if t == 0:
            tap("lnx0", fmbuf[:], (128, 8192), BF16)
        if stage in ("ln1", "ln1_nocast"):
            break


        def stream_in(gi):
            buf = wbufA if gi % 2 == 0 else wbufB
            tk.dma("qsp", buf[:].rearrange("p (o q) -> p o q", o=2),
                   bw_in[gi * 2 * 128:(gi * 2 + 2) * 128, :]
                   .rearrange("(o p) q -> p o q", p=128),
                   reads=("bw_in",), writes=(buf.name,),
                   ch="cwa" if gi % 2 == 0 else "cwb")
            return buf

        stream_in(0)
        stream_in(1)
        conv_par = [0]
        for oc in range(80):
            gi = oc // 2
            buf = wbufA if gi % 2 == 0 else wbufB
            off = (oc % 2) * 2048
            bank = oc % 2
            for dc in range(16):
                mm(bank, buf[:, off + dc * 128:off + (dc + 1) * 128],
                   fmbuf[:, dc * 512:(dc + 1) * 512],
                   start=(dc == 0), stop=(dc == 15),
                   rkeys=(buf.name, "fmbuf"))
            if oc < 32:
                zs = zstage0 if oc % 2 == 0 else zstage1
                evac(zs[:], pb[bank][:], (f"pb{bank}",), (zs.name,))
                tk.dma("qact", zscr[oc * 128:(oc + 1) * 128, :], zs[:],
                       reads=(zs.name,), writes=("zscr",), ch="czs")
            else:
                c48 = oc - 32
                xs = xbcs0 if c48 % 2 == 0 else xbcs1
                ca = convacc0 if c48 % 2 == 0 else convacc1
                tk.op("dve", lambda e, x_=xs: e.tensor_copy(x_[:, 3:515], pb[bank][:]),
                      reads=(f"pb{bank}",), writes=(xs.name,))
                tk.op("dve", lambda e, x_=xs: e.tensor_copy(
                    x_[:, 0:3], tail[:, c48 * 3:c48 * 3 + 3]),
                    reads=("tail",), writes=(xs.name,))
                tk.op("dve", lambda e, x_=xs: e.tensor_copy(
                    tail[:, c48 * 3:c48 * 3 + 3], x_[:, 512:515]),
                    reads=(xs.name,), writes=("tail",))
                ceng = "dve"
                conv_par[0] += 1
                tk.op(ceng, lambda e, x_=xs, a_=ca: e.tensor_scalar(
                    a_[:], x_[:, 0:512], c_convw[:, c48 * 4:c48 * 4 + 1],
                    c_convb[:, c48:c48 + 1], ALU.mult, ALU.add),
                    reads=(xs.name, "c_convw", "c_convb"), writes=(ca.name,))
                for kk in range(1, 4):
                    tk.op(ceng, lambda e, x_=xs, a_=ca, k_=kk: e.scalar_tensor_tensor(
                        a_[:], x_[:, k_:k_ + 512], c_convw[:, c48 * 4 + k_:c48 * 4 + k_ + 1],
                        a_[:], ALU.mult, ALU.add),
                        reads=(xs.name, "c_convw", ca.name), writes=(ca.name,))
                if c48 < 32:
                    dst = big1b[:, c48 * 512:(c48 + 1) * 512]
                    dk = "big1"
                elif c48 < 40:
                    g = c48 - 32
                    dst = b_fm[:, g * 512:(g + 1) * 512]
                    dk = "b_fm"
                else:
                    g = c48 - 40
                    dst = c_fm[:, g * 512:(g + 1) * 512]
                    dk = "c_fm"
                tk.op("act", lambda e, d_=dst, a_=ca: e.activation(d_, a_[:], AF.Silu),
                      reads=(ca.name,), writes=(dk,))
            if oc % 2 == 1 and gi + 2 < 40:
                stream_in(gi + 2)
        tk.dma("qsp", wbufA[:, 0:1024], bw_indt[:, :], reads=("bw_indt",),
               writes=("wbufA",), ch="cwa")
        for dc in range(16):
            mm(2, wbufA[:, dc * 64:(dc + 1) * 64], fmbuf[:, dc * 512:(dc + 1) * 512],
               start=(dc == 0), stop=(dc == 15), rkeys=("wbufA", "fmbuf"),
               out_sl=pb[2][0:64, :])
        tk.op("dve", lambda e: e.tensor_scalar(dts[:], pb[2][0:64, :], c_dtb[:],
                                               None, ALU.add),
              reads=("pb2", "c_dtb"), writes=("dts",))
        dtab = big3[0:64, 2048:2560]
        tk.op("act", lambda e: e.activation(dtab, dts[:], AF.Abs),
              reads=("dts",), writes=("big3",))
        tk.op("act", lambda e: e.activation(dtab, dtab, AF.Exp, scale=-1.0),
              reads=("big3",), writes=("big3",))
        tk.op("act", lambda e: e.activation(dtab, dtab, AF.Ln, bias=1.0),
              reads=("big3",), writes=("big3",))
        tk.op("dve", lambda e: e.tensor_scalar_max(dts[:], dts[:], 0.0),
              reads=("dts",), writes=("dts",))
        tk.op("dve", lambda e: e.tensor_add(dts[:], dts[:], dtab),
              reads=("dts", "big3"), writes=("dts",))
        warm(24)
        if t == 0:
            tap("z0", zscr, (32 * 128, 512), BF16)
            tap("xfm0", big1b[:, 0:16384], (128, 16384), BF16)
            tap("bfm0", b_fm[:], (128, 4096), BF16)
            tap("cfm0", c_fm[:], (128, 4096), BF16)
            tap("dts0", dts[:], (64, 512), F32)
        if stage == "inproj":
            break

        for cc in range(2):
            csl = slice(cc * 256, (cc + 1) * 256)
            tk.op("dve", lambda e: e.tensor_scalar(dA, dts[:, csl], c_acol[0:64, :],
                                                   None, ALU.mult),
                  reads=("dts", "c_acol"), writes=("big3",))
            tk.op("dve", lambda e: e.tensor_tensor_scan(
                dAcs[:], dA, c_zcol[0:64, 0:1].broadcast_to([64, 256]), 0.0,
                ALU.add, ALU.add),
                reads=("big3", "c_zcol"), writes=("dAcs",))
            tk.op("act", lambda e: e.activation(lndt, dts[:, csl], AF.Ln),
                  reads=("dts",), writes=("big3",))
            tk.op("dve", lambda e: e.tensor_sub(lndt, lndt, dAcs[:]),
                  reads=("big3", "dAcs"), writes=("big3",))
            tk.op("act", lambda e: e.activation(w_hm, dAcs[:], AF.Exp, scale=-1.0,
                                                bias=dAcs[:, 255:256]),
                  reads=("dAcs",), writes=("ytmp",))
            tk.op("dve", lambda e: e.tensor_mul(w_hm, w_hm, dts[:, csl]),
                  reads=("ytmp", "dts"), writes=("ytmp",))
            for (srcT, dstT, skey, key) in (
                    (dAcs[:], dAcsT, "dAcs", "dAcsT"),
                    (lndt, colpT, "big3", "colpT"),
                    (w_hm, wT, "ytmp", "wT")):
                for jc in range(2):
                    tp(7, srcT[:, jc * 128:(jc + 1) * 128], c_id[0:64, 0:64],
                       (skey,), pb[7][0:128, 0:64])
                    evac(dstT[:, jc * 64:(jc + 1) * 64], pb[7][0:128, 0:64],
                         ("pb7",), (key,))
            tp(7, dAcs[:, 255:256], c_id[0:64, 0:64], ("dAcs",), pb[7][0:1, 0:64])
            tk.op("dve", lambda e: e.tensor_copy(decrow[:], pb[7][0:1, 0:64]),
                  reads=("pb7",), writes=("decrow",))
            tk.op("act", lambda e: e.activation(decrow[:], decrow[:], AF.Exp),
                  reads=("decrow",), writes=("decrow",))
            tk.op("pe", lambda e: e.matmul(pb[7][0:128, 0:64], c_ones1f[:], decrow[:],
                                           start=True, stop=True),
                  reads=("c_ones1f", "decrow"), writes=("pb7",))
            tk.op("dve", lambda e: e.tensor_copy(decayexp[:], pb[7][0:128, 0:64]),
                  reads=("pb7",), writes=("decayexp",))
            warm(24)
            for jc in range(2):
                for half in range(4):
                    bk = 2 if (jc * 4 + half) % 2 == 0 else 3
                    for k in range(8):
                        c = half * 8 + k
                        tp(bk, big1b[:, c * 512 + cc * 256 + jc * 128:
                                     c * 512 + cc * 256 + (jc + 1) * 128],
                           c_idb[:], ("big1",), pbb[bk][:, k * 128:(k + 1) * 128])
                    dst = fmbuf[:, jc * 4096 + half * 1024:jc * 4096 + (half + 1) * 1024]
                    tk.op("dve", lambda e, d_=dst, b_=bk: e.tensor_copy(d_, pbb[b_][:, 0:1024]),
                          reads=(f"pb{bk}",), writes=("fmbuf",))

            for g in range(NG):
                gsl = slice(g * 512, (g + 1) * 512)
                dAcs8 = ytmp[0:8, 0:256]
                for jc in range(2):
                    tp(5, dAcsT[:, jc * 64 + g * 8:jc * 64 + (g + 1) * 8],
                       c_id[:], ("dAcsT",), pb[5][0:8, 0:128])
                    evac(dAcs8[:, jc * 128:(jc + 1) * 128], pb[5][0:8, 0:128],
                         ("pb5",), ("ytmp",))
                tk.op("act", lambda e: e.activation(e8g[:], dAcs8, AF.Exp),
                      reads=("ytmp",), writes=("e8g",))
                dup = lambda ap: ap.unsqueeze(1).broadcast_to([8, 2, 256])
                tk.op("dve", lambda e: e.tensor_copy(
                    dAcs8h[:].rearrange("p (r i) -> p r i", r=2), dup(dAcs8)),
                    reads=("ytmp",), writes=("dAcs8h",))
                tk.op("dve", lambda e: e.tensor_sub(dAcs8, dAcs8, dAcs8h[:, 0:256]),
                      reads=("ytmp", "dAcs8h"), writes=("ytmp",))
                tk.op("dve", lambda e: e.tensor_copy(
                    dAcs8m[:].rearrange("p (r i) -> p r i", r=2), dup(dAcs8)),
                    reads=("ytmp",), writes=("dAcs8m",))
                tk.op("dve", lambda e: e.tensor_sub(dAcs8, dAcs8, dAcs8m[:, 0:256]),
                      reads=("ytmp", "dAcs8m"), writes=("ytmp",))
                tk.op("dve", lambda e: e.tensor_copy(
                    dAcs8l[:].rearrange("p (r i) -> p r i", r=2), dup(dAcs8)),
                    reads=("ytmp",), writes=("dAcs8l",))
                tk.op("dve", lambda e: e.tensor_copy(sbf_g[:], S[:, gsl]),
                      reads=("S",), writes=("sbf_g",))
                for jc in range(2):
                    mm(5, b_fm[:, g * 512 + cc * 256 + jc * 128:
                               g * 512 + cc * 256 + (jc + 1) * 128],
                       c_fm[:, g * 512 + csl.start:g * 512 + csl.stop],
                       start=True, stop=True, rkeys=("b_fm", "c_fm"),
                       out_sl=pb[5][0:128, 0:256])
                    evac(cbT[:, jc * 256:(jc + 1) * 256], pb[5][0:128, 0:256],
                         ("pb5",), ("cbT",))
                for jc in range(2):
                    tp(5, b_fm[:, g * 512 + cc * 256 + jc * 128:
                               g * 512 + cc * 256 + (jc + 1) * 128],
                       c_idb[:], ("b_fm",), pbb[5][:, 0:128])
                    evac(bT[:, jc * 128:(jc + 1) * 128], pbb[5][:, 0:128],
                         ("pb5",), ("bT",))
                EBUF = [(expb0[:], "expb0"), (expb1[:], "expb1"),
                        (big2b[:, 1024:1280], "big2"), (big2b[:, 1280:1536], "big2")]
                MBUF = [(mT0[:], "mT0"), (mT1[:], "mT1"),
                        (big2b[:, 1536:1792], "big2"), (big2b[:, 1792:2048], "big2")]

                def emit_T1(hh):
                    t1bank = 0 if hh % 2 == 0 else 1
                    for i3, split in enumerate((dAcs8h, dAcs8m, dAcs8l)):
                        mm(t1bank, c_sel8[:, hh * 128:(hh + 1) * 128], split[:],
                           start=(i3 == 0), stop=False,
                           rkeys=("c_sel8", split.name),
                           out_sl=pb[t1bank][0:128, 0:512])
                    mm(t1bank, c_idb[:], c_mask[:],
                       start=False, stop=True, rkeys=("c_idb", "c_mask"),
                       out_sl=pb[t1bank][0:128, 0:512])

                emit_T1(0)
                for rc in range(4):
                    ydbank = 2 if rc % 2 == 0 else 3
                    for sub in range(2):
                        hh = 2 * rc + sub
                        h = g * 8 + hh
                        t1bank = 0 if hh % 2 == 0 else 1
                        if hh + 1 < 8:
                            emit_T1(hh + 1)
                        par = hh % 2
                        for jc in range(2):
                            eb, ebk = EBUF[2 * par + jc]
                            mt, mtk = MBUF[2 * par + jc]
                            tk.op("act", lambda e, x_=eb, tb=t1bank, j_=jc, h_=h:
                                  e.activation(x_, pb[tb][0:128, j_ * 256:(j_ + 1) * 256],
                                               AF.Exp,
                                               bias=colpT[:, j_ * 64 + h_:j_ * 64 + h_ + 1]),
                                  reads=(f"pb{t1bank}", "colpT"), writes=(ebk,))
                            tk.op("dve", lambda e, x_=eb, m_=mt, j_=jc:
                                  e.tensor_mul(m_, x_,
                                               cbT[:, j_ * 256:(j_ + 1) * 256]),
                                  reads=(ebk, "cbT"), writes=(mtk,))
                            mm(ydbank, fmbuf[:, jc * 4096 + h * 64:jc * 4096 + (h + 1) * 64],
                               mt, start=(jc == 0), stop=(jc == 1),
                               rkeys=("fmbuf", mtk),
                               out_sl=pb[ydbank][sub * 64:(sub + 1) * 64, 0:256])
                    warm(3)
                    yobank = 4
                    mm(yobank, sbf_g[:, rc * 128:(rc + 1) * 128],
                       c_fm[:, g * 512 + csl.start:g * 512 + csl.stop],
                       start=True, stop=True, rkeys=("sbf_g", "c_fm"),
                       out_sl=pb[yobank][0:128, 0:256])
                    mm(7, c_selpair[:, rc * 128:(rc + 1) * 128], e8g[:],
                       start=True, stop=True, rkeys=("c_selpair", "e8g"),
                       out_sl=pb[7][0:128, 0:256])
                    tk.op("dve", lambda e: e.tensor_copy(eb_s[:], pb[7][0:128, 0:256]),
                          reads=("pb7",), writes=("eb_s",))
                    tk.op("dve", lambda e, yb=yobank: e.tensor_mul(
                        ytmp[:], pb[yb][0:128, 0:256], eb_s[:]),
                        reads=(f"pb{yobank}", "eb_s"), writes=("ytmp",))
                    tk.op("dve", lambda e, yd=ydbank: e.tensor_add(
                        ytmp[:], ytmp[:], pb[yd][0:128, 0:256]),
                        reads=("ytmp", f"pb{ydbank}"), writes=("ytmp",))
                    ych = g * 4 + rc
                    ydst = big3b[:, ych * 256:(ych + 1) * 256]
                    tk.op("dve", lambda e, d_=ydst: e.scalar_tensor_tensor(
                        d_, big1b[:, ych * 512 + csl.start:ych * 512 + csl.stop],
                        c_dexp[:, g * 4 + rc:g * 4 + rc + 1], ytmp[:],
                        ALU.mult, ALU.add),
                        reads=("big1", "c_dexp", "ytmp"), writes=("big3",))
                for jc in range(2):
                    src3 = fmbuf[:, jc * 4096 + gsl.start:jc * 4096 + gsl.stop]
                    tk.op("dve", lambda e, s_=src3, j_=jc: e.tensor_tensor(
                        wx[:, j_ * 512:(j_ + 1) * 512].rearrange("p (h q) -> p h q", q=64),
                        s_.rearrange("p (h q) -> p h q", q=64),
                        wT[:, j_ * 64 + g * 8:j_ * 64 + (g + 1) * 8].unsqueeze(2)
                        .broadcast_to([128, 8, 64]),
                        ALU.mult),
                        reads=("fmbuf", "wT"), writes=("wx",))
                for jc in range(2):
                    mm(5, bT[:, jc * 128:(jc + 1) * 128], wx[:, jc * 512:(jc + 1) * 512],
                       start=(jc == 0), stop=(jc == 1), rkeys=("bT", "wx"))
                tk.op("dve", lambda e: e.tensor_tensor(
                    S[:, gsl].rearrange("p (h q) -> p h q", q=64),
                    S[:, gsl].rearrange("p (h q) -> p h q", q=64),
                    decayexp[:, g * 8:(g + 1) * 8].unsqueeze(2).broadcast_to([128, 8, 64]),
                    ALU.mult),
                    reads=("S", "decayexp"), writes=("S",))
                tk.op("dve", lambda e: e.tensor_add(S[:, gsl], S[:, gsl], pb[5][:]),
                      reads=("S", "pb5"), writes=("S",))

            if t == 0 and cc == 0:
                tap("y0", big3b[:, 0:8192], (128, 8192), BF16)

            def zload(cg, buf):
                tk.dma("qact", buf[:].rearrange("p (o i) -> p o i", i=256),
                       zscr[:].rearrange("(o p) t -> p o t", p=128)[
                           :, cg * 2:(cg + 1) * 2, csl],
                       reads=("zscr",), writes=(buf.name,),
                       ch="cza" if buf is zg4a else "czb")

            zload(0, zg4a)
            zload(1, zg4b)
            for c in range(32):
                ys = ysq0 if c % 2 == 0 else ysq1
                tk.op("act", lambda e, y_=ys, c_=c: e.activation(
                    y_[:], big3b[:, c_ * 256:(c_ + 1) * 256], AF.Square),
                    reads=("big3",), writes=(ys.name,))
                mm(0, c_onescolb[:], ys[:], start=(c == 0), stop=(c == 31),
                   rkeys=("c_onescolb", ys.name), out_sl=pb[0][0:1, 0:256])
            tk.op("dve", lambda e: e.tensor_scalar_mul(varrow[:], pb[0][0:1, 0:256],
                                                       1.0 / 4096),
                  reads=("pb0",), writes=("varrow",))
            tk.op("dve", lambda e: e.tensor_scalar(varrow[:], varrow[:],
                                                   c_eps[0:1, :], None, ALU.add),
                  reads=("varrow", "c_eps"), writes=("varrow",))
            tk.op("act", lambda e: e.activation(varrow[:], varrow[:], AF.Ln),
                  reads=("varrow",), writes=("varrow",))
            tk.op("act", lambda e: e.activation(rstdb[:], varrow[:], AF.Exp, scale=-0.5),
                  reads=("varrow",), writes=("rstdb",))
            warm(20)
            tk.op("pe", lambda e: e.matmul(pb[1][0:128, 0:256], c_ones1b[:], rstdb[:],
                                           start=True, stop=True),
                  reads=("c_ones1b", "rstdb"), writes=("pb1",))
            tk.op("dve", lambda e: e.tensor_copy(rb_s[:], pb[1][0:128, 0:256]),
                  reads=("pb1",), writes=("rb_s",))
            for c in range(32):
                cg = c // 2
                zbuf = zg4a if cg % 2 == 0 else zg4b
                ys = ysq0 if c % 2 == 0 else ysq1
                tk.op("dve", lambda e, c_=c: e.tensor_mul(
                    ytmp[:], big3b[:, c_ * 256:(c_ + 1) * 256], rb_s[:]),
                    reads=("big3", "rb_s"), writes=("ytmp",))
                tk.op("act", lambda e, y_=ys, z_=zbuf, c_=c: e.activation(
                    y_[:], z_[:, (c_ % 2) * 256:(c_ % 2 + 1) * 256], AF.Silu),
                    reads=(zbuf.name,), writes=(ys.name,))
                tk.op("dve", lambda e, y_=ys, c_=c: e.scalar_tensor_tensor(
                    big3b[:, 8192 + c_ * 256:8192 + (c_ + 1) * 256], ytmp[:],
                    c_normw[:, c_:c_ + 1], y_[:], ALU.mult, ALU.mult),
                    reads=("ytmp", "c_normw", ys.name, "big3"), writes=(f"yn{c}",))
                if c % 2 == 1 and cg + 2 < 16:
                    zload(cg + 2, zbuf)
            if t == 0 and cc == 0:
                tap("yn0", big3b[:, 8192:16384], (128, 8192), BF16)
            obuf = {0: None, 1: None}

            def stream_out(oc):
                buf = wbufA if oc % 2 == 0 else wbufB
                tk.dma("qsp", buf[:], bw_out[oc * 128:(oc + 1) * 128, :],
                       reads=("bw_out",), writes=(buf.name,),
                       ch="cwa" if oc % 2 == 0 else "cwb")
                return buf

            obuf[0] = stream_out(0)
            obuf[1] = stream_out(1)
            for oc in range(16):
                buf = obuf[oc % 2]
                bank = 4
                for ic in range(32):
                    mm(bank, buf[:, ic * 128:(ic + 1) * 128],
                       big3b[:, 8192 + ic * 256:8192 + (ic + 1) * 256],
                       start=(ic == 0), stop=(ic == 31),
                       rkeys=(buf.name, f"yn{ic}"), out_sl=pb[bank][0:128, 0:256])
                if oc + 2 < 16:
                    obuf[oc % 2] = stream_out(oc + 2)
                osb = os0 if oc % 2 == 0 else os1
                evac(osb[:], pb[bank][0:128, 0:256], (f"pb{bank}",), (osb.name,))
                slot = oc % 8
                for bb in range(2):
                    tb = 5 if bb == 0 else 7
                    tp(tb, osb[:, bb * 128:(bb + 1) * 128], c_idb[:],
                       (osb.name,), pbb[tb][:, slot * 128:(slot + 1) * 128])
                if oc % 8 == 7:
                    oc0 = oc - 7
                    for bb in range(2):
                        tb = 5 if bb == 0 else 7
                        b = 2 * cc + bb
                        dst = x_tm[:, b * 2048 + oc0 * 128:b * 2048 + (oc0 + 8) * 128]
                        tk.op("dve", lambda e, d_=dst, tb_=tb: e.tensor_add(
                            d_, d_, pbb[tb_][:, 0:1024]),
                            reads=("x_tm", f"pb{tb}"), writes=("x_tm",))
        if t == 0:
            tap("x1t0", x_tm[:], (128, 8192), F32)

        tk.alias("big3", "yn0", "yn1", "yn2", "yn3", "yn4", "yn5", "yn6", "yn7", "yn8", "yn9", "yn10", "yn11", "yn12", "yn13", "yn14", "yn15", "yn16", "yn17", "yn18", "yn19", "yn20", "yn21", "yn22", "yn23", "yn24", "yn25", "yn26", "yn27", "yn28", "yn29", "yn30", "yn31")
        for b in range(4):
            layernorm_block(b, sqv, "big3")
            warm(10)
            transpose_to_fm(None, b, tbank=6)
        for b in range(4):
            for dc in range(16):
                mm(0, fmbuf[:, dc * 512 + b * 128:dc * 512 + (b + 1) * 128],
                   c_gateb[:, dc * 8:(dc + 1) * 8],
                   start=(dc == 0), stop=(dc == 15), rkeys=("fmbuf", "c_gateb"),
                   out_sl=pb[0][0:128, 0:8])
            tk.op("dve", lambda e: e.tensor_copy(glog[:], pb[0][0:128, 0:8]),
                  reads=("pb0",), writes=("glog",))
            tk.op("dve", lambda e: e.reduce_max(gm1[:], glog[:], AX.X),
                  reads=("glog",), writes=("gm1",))
            tk.op("dve", lambda e: e.tensor_scalar_mul(gm1[:], gm1[:], -1.0),
                  reads=("gm1",), writes=("gm1",))
            tk.op("act", lambda e: e.activation(gsc[:], glog[:], AF.Exp, bias=gm1[:]),
                  reads=("glog", "gm1"), writes=("gsc",))
            tk.op("dve", lambda e: e.reduce_sum(gm1[:], gsc[:], AX.X),
                  reads=("gsc",), writes=("gm1",))
            tk.op("dve", lambda e: e.reciprocal(gm1[:], gm1[:]),
                  reads=("gm1",), writes=("gm1",))
            tk.op("dve", lambda e: e.tensor_scalar(gsc[:], gsc[:], gm1[:], None,
                                                   ALU.mult),
                  reads=("gsc", "gm1"), writes=("gsc",))
            tk.op("dve", lambda e: e.reduce_max(gm1[:], gsc[:], AX.X),
                  reads=("gsc",), writes=("gm1",))
            tk.op("dve", lambda e: e.tensor_scalar(gtmp[:], gsc[:], gm1[:], None,
                                                   ALU.is_ge),
                  reads=("gsc", "gm1"), writes=("gtmp",))
            tk.op("dve", lambda e: e.scalar_tensor_tensor(gtmp[:], gtmp[:], -1e30,
                                                          gsc[:], ALU.mult, ALU.add),
                  reads=("gtmp", "gsc"), writes=("gtmp",))
            tk.op("dve", lambda e: e.reduce_max(gm1[:], gtmp[:], AX.X),
                  reads=("gtmp",), writes=("gm1",))
            tk.op("dve", lambda e: e.tensor_scalar(gtmp[:], gsc[:], gm1[:], None,
                                                   ALU.is_ge),
                  reads=("gsc", "gm1"), writes=("gtmp",))
            tk.op("dve", lambda e: e.tensor_mul(gsc[:], gsc[:], gtmp[:]),
                  reads=("gsc", "gtmp"), writes=("gsc",))
            warm(10)
            tp(1, gsc[:, 0:8], c_id[:], ("gsc",), pb[1][0:8, 0:128])
            tk.op("dve", lambda e, b_=b: e.tensor_copy(
                comb_fm[:, b_ * 128:(b_ + 1) * 128], pb[1][0:8, 0:128]),
                reads=("pb1",), writes=("comb_fm",))
        if t == 0:
            tap("comb0", comb_fm[:], (8, 512), BF16)

        g_s = big2b[:, 0:5632]
        wdA = big3b[:, 0:5632]
        wdB = big3b[:, 5632:11264]
        tk.alias("wdA", "big3", "yn0", "yn1", "yn2", "yn3", "yn4", "yn5", "yn6", "yn7", "yn8", "yn9", "yn10", "yn11", "yn12", "yn13", "yn14", "yn15", "yn16", "yn17", "yn18", "yn19", "yn20", "yn21", "yn22", "yn23", "yn24", "yn25", "yn26", "yn27", "yn28", "yn29", "yn30", "yn31")
        tk.alias("wdB", "big3", "yn0", "yn1", "yn2", "yn3", "yn4", "yn5", "yn6", "yn7", "yn8", "yn9", "yn10", "yn11", "yn12", "yn13", "yn14", "yn15", "yn16", "yn17", "yn18", "yn19", "yn20", "yn21", "yn22", "yn23", "yn24", "yn25", "yn26", "yn27", "yn28", "yn29", "yn30", "yn31")

        def up_stage(hc, buf, gb, ub, comb_bank=None):
            for dc in range(16):
                mm(gb, buf[:, dc * 128:(dc + 1) * 128],
                   fmbuf[:, dc * 512:(dc + 1) * 512], start=(dc == 0),
                   stop=(dc == 15), rkeys=(buf.name, "fmbuf"))
            for dc in range(16):
                mm(ub, buf[:, 2048 + dc * 128:2048 + (dc + 1) * 128],
                   fmbuf[:, dc * 512:(dc + 1) * 512], start=(dc == 0),
                   stop=(dc == 15), rkeys=(buf.name, "fmbuf"))
            sl = hc % 11
            zz = zstage0 if hc % 2 == 0 else zstage1
            tk.op("act", lambda e, z_=zz, g_=gb: e.activation(z_[:], pb[g_][:], AF.Silu),
                  reads=(f"pb{gb}",), writes=(zz.name,))
            tk.op("dve", lambda e, z_=zz, u_=ub, s_=sl: e.tensor_tensor(
                g_s[:, s_ * 512:(s_ + 1) * 512], z_[:], pb[u_][:], ALU.mult),
                reads=(zz.name, f"pb{ub}"), writes=("big2",))
            if comb_bank is not None:
                tk.op("dve", lambda e, s_=sl: e.tensor_tensor(
                    g_s[:, s_ * 512:(s_ + 1) * 512],
                    g_s[:, s_ * 512:(s_ + 1) * 512],
                    pb[comb_bank][:], ALU.mult),
                    reads=("big2", f"pb{comb_bank}"), writes=("big2",))

        def sgu_dma(hc, buf):
            tk.dma("qsp", buf[:], bw_sgu[hc * 128:(hc + 1) * 128, :],
                   reads=("bw_sgu",), writes=(buf.name,),
                   ch="cwa" if buf is wbufA else "cwb")

        def sd_dma(dg, buf, bkey):
            tk.dma("qsp", buf, bw_sd[dg * 128:(dg + 1) * 128, :],
                   reads=("bw_sd",), writes=(bkey,),
                   ch="cda" if bkey == "wdA" else "cdb")

        bufs2 = [wbufA, wbufB]
        for sp_pass in range(2):
            h0 = sp_pass * 11
            sgu_dma(h0 + 0, bufs2[0])
            sgu_dma(h0 + 1, bufs2[1])
            for hl in range(11):
                hc = h0 + hl
                buf = bufs2[hl % 2]
                gb = 2 if hl % 2 == 0 else 4
                ub = 3 if hl % 2 == 0 else 6
                up_stage(hc, buf, gb, ub)
                if hl + 2 < 11:
                    sgu_dma(h0 + hl + 2, bufs2[hl % 2])
            sd_dma(0, wdA, "wdA")
            sd_dma(1, wdB, "wdB")
            for dg in range(8):
                buf, bkey = (wdA, "wdA") if dg % 2 == 0 else (wdB, "wdB")
                for sub in range(2):
                    dc = dg * 2 + sub
                    bank = 0 if dc % 2 == 0 else 1
                    for hl in range(11):
                        hc = h0 + hl
                        mm(bank, buf[:, sub * 2816 + hc * 128:sub * 2816 + (hc + 1) * 128],
                           g_s[:, hl * 512:(hl + 1) * 512], start=(hl == 0),
                           stop=(hl == 10), rkeys=(bkey, "big2"))
                    if sp_pass == 0:
                        tk.op("dve", lambda e, dc_=dc, b_=bank: e.tensor_copy(
                            big1[:, dc_ * 512:(dc_ + 1) * 512], pb[b_][:]),
                            reads=(f"pb{bank}",), writes=("big1",))
                    else:
                        tk.op("dve", lambda e, dc_=dc, b_=bank: e.tensor_add(
                            big1[:, dc_ * 512:(dc_ + 1) * 512],
                            big1[:, dc_ * 512:(dc_ + 1) * 512], pb[b_][:]),
                            reads=("big1", f"pb{bank}"), writes=("big1",))
                if dg + 2 < 8:
                    sd_dma(dg + 2, buf, bkey)

        def gu_dma(ex_, hc, buf):
            row0 = (ex_ * 11 + hc) * 128
            tk.dma("qsp", buf[:], bw_gu[row0:row0 + 128, :],
                   reads=("bw_gu",), writes=(buf.name,),
                   ch="cwa" if buf is wbufA else "cwb")

        def d_dma(ex_, dg, buf, bkey):
            row0 = (ex_ * 4 + dg) * 128
            tk.dma("qsp", buf, bw_d[row0:row0 + 128, :],
                   reads=("bw_d",), writes=(bkey,),
                   ch="cda" if bkey == "wdA" else "cdb")

        for ex in range(EXP):
            tk.op("pe", lambda e, x_=ex: e.matmul(
                pb[7][0:128, 0:512], c_sel8[:, x_ * 128:(x_ + 1) * 128], comb_fm[:],
                start=True, stop=True),
                reads=("c_sel8", "comb_fm"), writes=("pb7",))
            gu_dma(ex, 0, wbufA)
            gu_dma(ex, 1, wbufB)
            for hc in range(11):
                buf = bufs2[hc % 2]
                gb = 2 if hc % 2 == 0 else 4
                ub = 3 if hc % 2 == 0 else 5
                up_stage(hc, buf, gb, ub, comb_bank=7)
                if hc + 2 < 11:
                    gu_dma(ex, hc + 2, bufs2[hc % 2])
            d_dma(ex, 0, wdA, "wdA")
            d_dma(ex, 1, wdB, "wdB")
            for dg in range(4):
                buf, bkey = (wdA, "wdA") if dg % 2 == 0 else (wdB, "wdB")
                for sub in range(4):
                    dc = dg * 4 + sub
                    bank = 0 if dc % 2 == 0 else 1
                    for hc in range(11):
                        mm(bank, buf[:, sub * 1408 + hc * 128:sub * 1408 + (hc + 1) * 128],
                           g_s[:, hc * 512:(hc + 1) * 512], start=(hc == 0),
                           stop=(hc == 10), rkeys=(bkey, "big2"))
                    tk.op("dve", lambda e, dc_=dc, b_=bank: e.tensor_add(
                        big1[:, dc_ * 512:(dc_ + 1) * 512],
                        big1[:, dc_ * 512:(dc_ + 1) * 512], pb[b_][:]),
                        reads=("big1", f"pb{bank}"), writes=("big1",))
                if dg + 2 < 4:
                    d_dma(ex, dg + 2, buf, bkey)

        for b in range(4):
            for half in range(4):
                for k in range(4):
                    dc = half * 4 + k
                    tp(5, big1[:, dc * 512 + b * 128:dc * 512 + (b + 1) * 128],
                       c_id[:], ("big1",), pb[5][:, k * 128:(k + 1) * 128])
                dst = x_tm[:, b * 2048 + half * 512:b * 2048 + (half + 1) * 512]
                tk.op("dve", lambda e, d_=dst: e.tensor_add(d_, d_, pb[5][:]),
                      reads=("x_tm", "pb5"), writes=("x_tm",))
            tk.dma("qact", out[T0 + b * 128:T0 + (b + 1) * 128, :],
                   x_tm[:, b * 2048:(b + 1) * 2048],
                   reads=("x_tm",), writes=("out",), ch="cout")

    tk.finish()
    nc.finalize()
    return nc



_CACHE = {}


BUILD_TILES = N_TILES
BUILD_STAGE = "full"


def kernel(**inputs):
    const, xs = prep_inputs(inputs)
    nc = build_program(BUILD_TILES, tuple(DEBUG_TAPS), BUILD_STAGE)
    in_maps = []
    for c in range(8):
        m = dict(const)
        m["xin"] = xs[c]
        in_maps.append(m)
    res = run_bass_kernel_spmd(nc, in_maps, core_ids=list(range(8)))
    outs = np.stack([res.results[c]["out"] for c in range(8)], axis=0)
    if DEBUG_TAPS:
        kernel.taps = [{k.replace("tap_", ""): v for k, v in r.items()
                        if k.startswith("tap_")} for r in res.results]
    return outs


# version 13
# speedup vs baseline: 1.0994x; 1.0742x over previous
import numpy as np
import ml_dtypes
import concourse.bass as bass
import concourse.mybir as mybir
from concourse.bass_utils import run_bass_kernel_spmd

F32 = mybir.dt.float32
BF16 = mybir.dt.bfloat16
AF = mybir.ActivationFunctionType
ALU = mybir.AluOpType
AX = mybir.AxisListType
BF = ml_dtypes.bfloat16

HID = 2048
INTER = 4096
CONVD = 6144
PROJ = 10304
NH, PDIM, NG, NST, CS = 64, 64, 8, 128, 256
LSEQ = 4096
TILE = 512
EXP, HEXP, HSH = 8, 1408, 2816
EPS = 1e-5

N_TILES = LSEQ // TILE
DEBUG_TAPS = []

import bisect


class TK:

    def __init__(self, nc):
        self.nc = nc
        self.E = {"pe": nc.tensor, "dve": nc.vector, "act": nc.scalar,
                  "pool": nc.gpsimd, "sp": nc.sync}
        self.QENG = {"qsp": "sp", "qact": "act", "qpool": "pool"}
        self.sem = {k: nc.alloc_semaphore("sem_" + k)
                    for k in ["pe", "dve", "act", "pool"]}
        self.n = {k: 0 for k in self.E}
        self.insts = {k: [] for k in self.E}
        self.isdma = {k: [] for k in self.E}
        self.incs = {k: [] for k in ["pe", "dve", "act", "pool"]}
        self.qcnt = {}
        self.lastw = {}
        self.readers = {}
        self.waited = {k: {} for k in self.E}

    def chan(self, name):
        if name not in self.qcnt:
            self.qcnt[name] = 0
            self.sem[name] = self.nc.alloc_semaphore("sem_" + name)
        return name

    def _wait(self, stream, semk, val):
        if val <= self.waited[stream].get(semk, 0):
            return
        self.E[stream].wait_ge(self.sem[semk], val)
        self.waited[stream][semk] = val

    def _need(self, stream, src):
        if src[0] == "q":
            _, ch, cnt = src
            self._wait(stream, ch, 16 * cnt)
        else:
            _, eng, idx = src
            if eng == stream and eng == "pe":
                return
            arr = self.incs[eng]
            i = bisect.bisect_left(arr, (idx, -1))
            if i < len(arr):
                pos, cum = arr[i]
            else:
                assert not self.isdma[eng][idx], "inc attach on dma instr"
                h = self.insts[eng][idx]
                cum = (arr[-1][1] + 1) if arr else 1
                h.then_inc(self.sem[eng], 1)
                arr.append((idx, cum))
            self._wait(stream, eng, cum)

    def _pre(self, stream, reads, writes):
        for k in reads:
            lw = self.lastw.get(k)
            if lw is not None:
                self._need(stream, lw)
        for k in writes:
            lw = self.lastw.get(k)
            if lw is not None:
                self._need(stream, lw)
            for src in self.readers.get(k, {}).values():
                self._need(stream, src)

    def _post(self, me, srcname, reads, writes):
        for k in reads:
            self.readers.setdefault(k, {})[srcname] = me
        for k in writes:
            self.lastw[k] = me
            self.readers[k] = {}

    def op(self, stream, emit, reads=(), writes=()):
        self._pre(stream, reads, writes)
        h = emit(self.E[stream])
        idx = self.n[stream]
        self.insts[stream].append(h)
        self.isdma[stream].append(False)
        self.n[stream] += 1
        self._post(("e", stream, idx), stream, reads, writes)
        return h

    def dma(self, q, out_ap, in_ap, reads=(), writes=(), ch=None, **kw):
        stream = self.QENG[q]
        assert ch is not None
        self.chan(ch)
        self._pre(stream, reads, writes)
        h = self.E[stream].dma_start(out_ap, in_ap, **kw)
        self.insts[stream].append(h)
        self.isdma[stream].append(True)
        self.n[stream] += 1
        self.qcnt[ch] += 1
        h.then_inc(self.sem[ch], 16)
        self._post(("q", ch, self.qcnt[ch]), ch, reads, writes)
        return h

    def alias(self, dst, *srcs):
        best = None
        rd = {}
        for s in srcs:
            lw = self.lastw.get(s)
            if lw is not None:
                self.lastw.setdefault(dst, None)
                rd[("w",) + lw] = lw
            for name, src in self.readers.get(s, {}).items():
                rd[("r", name, s)] = src
        if rd:
            d = self.readers.setdefault(dst, {})
            for kk, src in rd.items():
                d[kk] = src

    def finish(self):
        for ch in list(self.qcnt):
            self._wait("sp", ch, 16 * self.qcnt[ch])
        for eng in ["pe", "dve", "act", "pool"]:
            i = self.n[eng] - 1
            while i >= 0 and self.isdma[eng][i]:
                i -= 1
            if i >= 0:
                self._need("sp", ("e", eng, i))




def _tile2d(w, rows, cols, rtile=128, ctile=128):
    rt, ct = rows // rtile, cols // ctile
    g = w.reshape(rt, rtile, ct, ctile).transpose(2, 1, 0, 3)
    return np.ascontiguousarray(g.reshape(ct * rtile, rt * ctile))


def prep_inputs(ins):
    d = {}
    f32 = np.float32
    W_inT = np.ascontiguousarray(ins["L0_attn_in_proj_w"].T)
    d["w_in"] = _tile2d(W_inT[:, :10240], 2048, 10240)
    d["w_indt"] = _tile2d(W_inT[:, 10240:], 2048, 64, ctile=64)
    W_outT = np.ascontiguousarray(ins["L0_attn_out_proj_w"].T)
    d["w_out"] = _tile2d(W_outT, 4096, 2048)

    wg, wu, wd = ins["L0_ffn_w_gate"], ins["L0_ffn_w_up"], ins["L0_ffn_w_down"]
    gu = []
    for e in range(EXP):
        tg = _tile2d(wg[e], 2048, 1408)
        tu = _tile2d(wu[e], 2048, 1408)
        gu.append(np.concatenate(
            [tg.reshape(11, 128, 2048), tu.reshape(11, 128, 2048)], axis=2))
    d["w_gu"] = np.ascontiguousarray(
        np.stack(gu).reshape(EXP * 11 * 128, 4096))
    wdt = []
    for e in range(EXP):
        t = _tile2d(wd[e], 1408, 2048)
        wdt.append(t.reshape(4, 4, 128, 1408).transpose(0, 2, 1, 3).reshape(4 * 128, 4 * 1408))
    d["w_d"] = np.ascontiguousarray(np.stack(wdt).reshape(EXP * 4 * 128, 4 * 1408))

    tsg = _tile2d(ins["L0_ffn_sw_gate"], 2048, 2816)
    tsu = _tile2d(ins["L0_ffn_sw_up"], 2048, 2816)
    d["w_sgu"] = np.ascontiguousarray(np.concatenate(
        [tsg.reshape(22, 128, 2048), tsu.reshape(22, 128, 2048)], axis=2
    ).reshape(22 * 128, 4096))
    tsd = _tile2d(ins["L0_ffn_sw_down"], 2816, 2048)
    d["w_sd"] = np.ascontiguousarray(
        tsd.reshape(8, 2, 128, 2816).transpose(0, 2, 1, 3).reshape(8 * 128, 2 * 2816))

    d["w_gate"] = np.ascontiguousarray(
        ins["L0_ffn_gate_w"].reshape(16, 128, 8).transpose(1, 0, 2).reshape(128, 128))

    d["conv_w_t"] = np.ascontiguousarray(
        ins["L0_attn_conv_w"].reshape(48, 128, 4).transpose(1, 0, 2).reshape(128, 192))
    d["conv_b_t"] = np.ascontiguousarray(ins["L0_attn_conv_b"].reshape(48, 128).T)
    d["norm_w_t"] = np.ascontiguousarray(ins["L0_attn_norm_w"].reshape(32, 128).T)
    d["dt_bias_c"] = ins["L0_attn_dt_bias"].reshape(64, 1).astype(f32)
    d["a_log_c"] = ins["L0_attn_A_log"].reshape(64, 1).astype(f32)
    pp = np.arange(128)[:, None, None] // 64
    gg = np.arange(8)[None, :, None]
    rr = np.arange(4)[None, None, :]
    idx = gg * 8 + 2 * rr + pp
    d["d_exp"] = np.ascontiguousarray(
        ins["L0_attn_D"][idx.reshape(128, 32)]).astype(f32)

    d["id128"] = np.eye(128, dtype=f32)
    d["id128b"] = np.eye(128, dtype=f32).astype(BF)
    d["ones1b"] = np.ones((1, 128), dtype=f32).astype(BF)
    d["ones1f"] = np.ones((1, 128), dtype=f32)
    d["onescolb"] = np.ones((128, 1), dtype=f32).astype(BF)
    sel8 = np.zeros((8, 8 * 128), dtype=f32)
    for h in range(8):
        sel8[h, h * 128:(h + 1) * 128] = 1.0
    d["sel8"] = sel8.astype(BF)
    selp = np.zeros((8, 4 * 128), dtype=f32)
    for rc in range(4):
        selp[2 * rc, rc * 128:rc * 128 + 64] = 1.0
        selp[2 * rc + 1, rc * 128 + 64:rc * 128 + 128] = 1.0
    d["selpair"] = selp.astype(BF)
    mask = np.zeros((128, 512), dtype=f32)
    for jc in range(2):
        for p in range(128):
            for i in range(256):
                if i < jc * 128 + p:
                    mask[p, jc * 256 + i] = -1e30
    d["maskneg"] = mask.astype(BF)
    d["zcol"] = np.zeros((128, 1), dtype=f32)
    d["epscol"] = np.full((128, 1), EPS, dtype=f32)

    xs = [np.ascontiguousarray(ins["L0_attn_hidden_states"][c]) for c in range(8)]
    return d, xs




def build_program(n_tiles=N_TILES, taps=(), stage="full"):
    nc = bass.Bass("TRN2", target_bir_lowering=False)
    tk = TK(nc)
    for _c in ["cc", "ci", "cid", "co", "cgu", "cd", "csg", "csd", "cx",
               "cwa", "cwb", "cda", "cdb", "cza", "czb", "czs", "cout", "ctap"]:
        tk.chan(_c)
    for _s in tk.sem.values():
        nc.gpsimd.sem_clear(_s)
    nc.all_engine_barrier()

    def din(name, shape, dt=F32):
        return nc.dram_tensor(name, list(shape), dt, kind="ExternalInput").ap()

    xin = din("xin", (LSEQ, HID))
    w_in = din("w_in", (80 * 128, 2048))
    w_indt = din("w_indt", (128, 1024))
    w_out = din("w_out", (16 * 128, 4096))
    w_gu = din("w_gu", (EXP * 11 * 128, 4096))
    w_d = din("w_d", (EXP * 4 * 128, 4 * 1408))
    w_sgu = din("w_sgu", (22 * 128, 4096))
    w_sd = din("w_sd", (8 * 128, 2 * 2816))
    w_gate = din("w_gate", (128, 128))
    conv_w_t = din("conv_w_t", (128, 192))
    conv_b_t = din("conv_b_t", (128, 48))
    norm_w_t = din("norm_w_t", (128, 32))
    dt_bias_c = din("dt_bias_c", (64, 1))
    a_log_c = din("a_log_c", (64, 1))
    d_exp = din("d_exp", (128, 32))
    id128 = din("id128", (128, 128))
    id128b = din("id128b", (128, 128), BF16)
    ones1b = din("ones1b", (1, 128), BF16)
    ones1f = din("ones1f", (1, 128))
    onescolb = din("onescolb", (128, 1), BF16)
    sel8 = din("sel8", (8, 1024), BF16)
    selpair = din("selpair", (8, 512), BF16)
    maskneg = din("maskneg", (128, 512), BF16)
    zcol = din("zcol", (128, 1))
    epscol = din("epscol", (128, 1))

    out = nc.dram_tensor("out", [LSEQ, HID], F32, kind="ExternalOutput").ap()

    def scr(name, shape):
        return nc.dram_tensor(name, list(shape), BF16, kind="Internal").ap()

    bw_in = scr("bw_in", (80 * 128, 2048))
    bw_indt = scr("bw_indt", (128, 1024))
    bw_out = scr("bw_out", (16 * 128, 4096))
    bw_gu = scr("bw_gu", (EXP * 11 * 128, 4096))
    bw_d = scr("bw_d", (EXP * 4 * 128, 4 * 1408))
    bw_sgu = scr("bw_sgu", (22 * 128, 4096))
    bw_sd = scr("bw_sd", (8 * 128, 2 * 2816))
    zscr = scr("zscr", (32 * 128, 512))

    def sb(name, shape, dt):
        return nc.alloc_sbuf_tensor(name, list(shape), dt)

    x_tm = sb("x_tm", (128, 4 * 2048), F32)
    fmbuf = sb("fmbuf", (128, 8192), BF16)
    big1 = sb("big1", (128, 8192), F32)
    b_fm = sb("b_fm", (128, 4096), BF16)
    c_fm = sb("c_fm", (128, 4096), BF16)
    big2 = sb("big2", (128, 2816), F32)
    big3 = sb("big3", (128, 8192), F32)
    S = sb("S", (128, 4096), F32)
    sbf_g = sb("sbf_g", (128, 512), BF16)
    wbufA = sb("wbufA", (128, 4096), BF16)
    wbufB = sb("wbufB", (128, 4096), BF16)
    xbcs0 = sb("xbcs0", (128, 516), BF16)
    xbcs1 = sb("xbcs1", (128, 516), BF16)
    convacc0 = sb("convacc0", (128, 512), BF16)
    convacc1 = sb("convacc1", (128, 512), BF16)
    tail = sb("tail", (128, 144), BF16)
    dts = sb("dts", (64, 512), F32)
    dAcs = sb("dAcs", (64, 256), F32)
    dAcsT = sb("dAcsT", (128, 128), F32)
    colpT = sb("colpT", (128, 128), F32)
    wT = sb("wT", (128, 128), F32)
    dAcs8h = sb("dAcs8h", (8, 512), BF16)
    dAcs8m = sb("dAcs8m", (8, 512), BF16)
    dAcs8l = sb("dAcs8l", (8, 512), BF16)
    e8g = sb("e8g", (8, 256), BF16)
    decrow = sb("decrow", (1, 64), F32)
    decayexp = sb("decayexp", (128, 64), F32)
    cbT = sb("cbT", (128, 512), BF16)
    bT = sb("bT", (128, 256), BF16)
    wx = sb("wx", (128, 1024), BF16)
    expb0 = sb("expb0", (128, 256), BF16)
    expb1 = sb("expb1", (128, 256), BF16)
    mT0 = sb("mT0", (128, 256), BF16)
    mT1 = sb("mT1", (128, 256), BF16)
    eb_s = sb("eb_s", (128, 256), BF16)
    ytmp = sb("ytmp", (128, 256), F32)
    ysq0 = sb("ysq0", (128, 256), BF16)
    ysq1 = sb("ysq1", (128, 256), BF16)
    varrow = sb("varrow", (1, 256), F32)
    rstdb = sb("rstdb", (1, 256), BF16)
    rb_s = sb("rb_s", (128, 256), BF16)
    os0 = sb("os0", (128, 256), BF16)
    os1 = sb("os1", (128, 256), BF16)
    colm = sb("colm", (128, 1), F32)
    colv = sb("colv", (128, 1), F32)
    colt = sb("colt", (128, 1), F32)
    zg4a = sb("zg4a", (128, 512), BF16)
    zg4b = sb("zg4b", (128, 512), BF16)
    glog = sb("glog", (128, 8), F32)
    gsc = sb("gsc", (128, 8), F32)
    gtmp = sb("gtmp", (128, 8), F32)
    gm1 = sb("gm1", (128, 1), F32)
    comb_fm = sb("comb_fm", (8, 512), BF16)
    c_convw = sb("c_convw", (128, 192), F32)
    c_convb = sb("c_convb", (128, 48), F32)
    c_normw = sb("c_normw", (128, 32), F32)
    c_dtb = sb("c_dtb", (64, 1), F32)
    c_acol = sb("c_acol", (64, 1), F32)
    c_dexp = sb("c_dexp", (128, 32), F32)
    c_id = sb("c_id", (128, 128), F32)
    c_idb = sb("c_idb", (128, 128), BF16)
    c_ones1b = sb("c_ones1b", (1, 128), BF16)
    c_ones1f = sb("c_ones1f", (1, 128), F32)
    c_onescolb = sb("c_onescolb", (128, 1), BF16)
    c_sel8 = sb("c_sel8", (8, 1024), BF16)
    c_selpair = sb("c_selpair", (8, 512), BF16)
    c_mask = sb("c_mask", (128, 512), BF16)
    c_zcol = sb("c_zcol", (128, 1), F32)
    c_eps = sb("c_eps", (128, 1), F32)
    c_gateb = sb("c_gateb", (128, 128), BF16)

    big1b = big1[:].bitcast(BF16)
    big2b = big2[:].bitcast(BF16)
    big3b = big3[:].bitcast(BF16)

    pb = [nc.alloc_psum_tensor(f"pb{i}", [128, 512], F32) for i in range(8)]
    pbb = [p[:].bitcast(BF16) for p in pb]

    lnstage = big3b[:, 8192:10240]
    zstage0, zstage1 = convacc0, convacc1
    dA = big3[0:64, 2560:2816]
    lndt = big3[0:64, 2816:3072]
    w_hm = ytmp[0:64, 0:256]

    def K(ap_or_name):
        return ap_or_name if isinstance(ap_or_name, str) else ap_or_name.name

    def mm(bank, lhsT, rhs, start, stop, rkeys, out_sl=None):
        o = pb[bank][:] if out_sl is None else out_sl
        tk.op("pe", lambda e: e.matmul(o, lhsT, rhs, start=start, stop=stop),
              reads=tuple(rkeys), writes=(f"pb{bank}",))

    def tp(bank, in_ap, ident, rkeys, out_sl):
        tk.op("pe", lambda e: e.transpose(out_sl, in_ap, ident),
              reads=tuple(rkeys), writes=(f"pb{bank}",))

    taps_out = {}

    def tap(name, ap, shape, dt=F32):
        if name not in taps:
            return
        t = nc.dram_tensor("tap_" + name, list(shape), dt, kind="ExternalOutput").ap()
        tk._wait("act", "ctap", 16 * tk.qcnt["ctap"])
        tk.dma("qact", t, ap, reads=(ap.name,), ch="ctap")
        taps_out[name] = True

    for dst, src in [
        (c_convw[:], conv_w_t), (c_convb[:], conv_b_t), (c_normw[:], norm_w_t),
        (c_dtb[:], dt_bias_c), (c_dexp[:], d_exp), (c_id[:], id128),
        (c_idb[:], id128b), (c_ones1b[:], ones1b), (c_ones1f[:], ones1f),
        (c_onescolb[:], onescolb), (c_sel8[:], sel8), (c_selpair[:], selpair),
        (c_mask[:], maskneg), (c_zcol[:], zcol), (c_eps[:], epscol),
    ]:
        tk.dma("qsp", dst, src, writes=(dst.name,), ch="cc")
    tk.dma("qsp", colt[0:64, :], a_log_c, writes=("colt",), ch="cc")
    tk.dma("qsp", big3[:, 0:128], w_gate, writes=("big3",), ch="cc")
    _cq = 16 * tk.qcnt["cc"]
    for _eng in ["pe", "dve", "act", "pool", "sp"]:
        tk._wait(_eng, "cc", _cq)
    tk.op("act", lambda e: e.activation(c_acol[0:64, :], colt[0:64, :], AF.Exp),
          reads=("colt",), writes=("c_acol",))
    tk.op("dve", lambda e: e.tensor_scalar_mul(c_acol[0:64, :], c_acol[0:64, :], -1.0),
          reads=("c_acol",), writes=("c_acol",))
    tk.op("dve", lambda e: e.tensor_copy(c_gateb[:], big3[:, 0:128]),
          reads=("big3",), writes=("c_gateb",))
    tk.op("dve", lambda e: e.memset(S[:], 0.0), writes=("S",))
    tk.op("dve", lambda e: e.memset(tail[:], 0.0), writes=("tail",))

    def cast_chunk(ch, key, dst, src, r0, r1):
        tk.dma("qpool", dst[r0:r1, :], src[r0:r1, :],
               reads=(src.name,), writes=(key,), ch=tk.chan(ch))

    if stage != "ln1_nocast":
        for k in range(5):
            cast_chunk(f"ci{k}", f"bw_in{k}", bw_in, w_in, k * 2048,
                       (k + 1) * 2048)
        cast_chunk("cid", "bw_indt", bw_indt, w_indt, 0, 128)
        for k in range(2):
            cast_chunk(f"co{k}", f"bw_out{k}", bw_out, w_out, k * 1024,
                       (k + 1) * 1024)
        for k in range(2):
            cast_chunk(f"csg{k}", f"bw_sgu{k}", bw_sgu, w_sgu, k * 1408,
                       (k + 1) * 1408)
        for k in range(2):
            cast_chunk(f"csd{k}", f"bw_sd{k}", bw_sd, w_sd, k * 512,
                       (k + 1) * 512)
        for ex_ in range(EXP):
            cast_chunk(f"cgu{ex_}", f"bw_gu{ex_}", bw_gu, w_gu,
                       ex_ * 11 * 128, (ex_ + 1) * 11 * 128)
            cast_chunk(f"cd{ex_}", f"bw_d{ex_}", bw_d, w_d,
                       ex_ * 512, (ex_ + 1) * 512)

    evac_rr = [0]

    def evac(dst_ap, src_ap, rkeys, wkeys):
        eng = "dve" if evac_rr[0] % 2 == 0 else "act"
        evac_rr[0] += 1
        if eng == "dve":
            tk.op("dve", lambda e: e.tensor_copy(dst_ap, src_ap),
                  reads=tuple(rkeys), writes=tuple(wkeys))
        else:
            tk.op("act", lambda e: e.copy(dst_ap, src_ap),
                  reads=tuple(rkeys), writes=tuple(wkeys))

    def layernorm_block(b, sq_scr, sq_key):
        xb = x_tm[:, b * 2048:(b + 1) * 2048]
        tk.op("dve", lambda e: e.reduce_sum(colm[:], xb, AX.X),
              reads=("x_tm",), writes=("colm",))
        tk.op("act", lambda e: e.activation(sq_scr, xb, AF.Square, accum_out=colv[:]),
              reads=("x_tm",), writes=(sq_key, "colv"))
        tk.op("dve", lambda e: e.tensor_scalar_mul(colm[:], colm[:], 1.0 / 2048),
              reads=("colm",), writes=("colm",))
        tk.op("dve", lambda e: e.tensor_scalar_mul(colv[:], colv[:], 1.0 / 2048),
              reads=("colv",), writes=("colv",))
        tk.op("dve", lambda e: e.tensor_mul(colt[:], colm[:], colm[:]),
              reads=("colm",), writes=("colt",))
        tk.op("dve", lambda e: e.tensor_sub(colv[:], colv[:], colt[:]),
              reads=("colv", "colt"), writes=("colv",))
        tk.op("dve", lambda e: e.tensor_scalar(colv[:], colv[:], c_eps[:], None,
                                               ALU.add),
              reads=("colv", "c_eps"), writes=("colv",))
        tk.op("act", lambda e: e.activation(colv[:], colv[:], AF.Ln),
              reads=("colv",), writes=("colv",))
        tk.op("act", lambda e: e.activation(colv[:], colv[:], AF.Exp, scale=-0.5),
              reads=("colv",), writes=("colv",))
        tk.op("dve", lambda e: e.tensor_scalar(lnstage, xb, colm[:], colv[:],
                                               ALU.subtract, ALU.mult),
              reads=("x_tm", "colm", "colv"), writes=("big3",))

    def transpose_to_fm(dst_col_of_chunk, b, tbank):
        for half in range(2):
            bk = tbank if (b * 2 + half) % 2 == 0 else 7
            for k in range(8):
                c = half * 8 + k
                tp(bk, lnstage[:, c * 128:(c + 1) * 128], c_idb[:],
                   ("big3",), pbb[bk][:, k * 128:(k + 1) * 128])
            dst = fmbuf[:].rearrange("p (c t) -> p c t", t=512)[
                :, half * 8:half * 8 + 8, b * 128:(b + 1) * 128]
            tk.op("dve", lambda e, d=dst, s=pbb[bk][:, 0:1024]: e.tensor_copy(d, s),
                  reads=(f"pb{bk}",), writes=("fmbuf",))

    sqv = big3[:, 0:2048]

    for t in range(n_tiles):
        T0 = t * TILE
        tk.alias("big3", "wdA", "wdB", "yn0", "yn1", "yn2", "yn3", "yn4", "yn5", "yn6", "yn7", "yn8", "yn9", "yn10", "yn11", "yn12", "yn13", "yn14", "yn15", "yn16", "yn17", "yn18", "yn19", "yn20", "yn21", "yn22", "yn23", "yn24", "yn25", "yn26", "yn27", "yn28", "yn29", "yn30", "yn31")
        for b in range(4):
            tk.dma("qact", x_tm[:, b * 2048:(b + 1) * 2048],
                   xin[T0 + b * 128:T0 + (b + 1) * 128, :],
                   reads=("xin",), writes=("x_tm",), ch="cx")
        if t == 0:
            tap("xin0", x_tm[:], (128, 8192), F32)
        for b in range(4):
            layernorm_block(b, sqv, "big3")
            if t == 0 and b == 0:
                tap("colm0", colm[:], (128, 1), F32)
                tap("colv0", colv[:], (128, 1), F32)
                tap("lnst0", lnstage, (128, 2048), BF16)
            transpose_to_fm(None, b, tbank=6)
        if t == 0:
            tap("lnx0", fmbuf[:], (128, 8192), BF16)
        if stage in ("ln1", "ln1_nocast"):
            break


        def stream_in(gi):
            buf = wbufA if gi % 2 == 0 else wbufB
            tk.dma("qsp", buf[:].rearrange("p (o q) -> p o q", o=2),
                   bw_in[gi * 2 * 128:(gi * 2 + 2) * 128, :]
                   .rearrange("(o p) q -> p o q", p=128),
                   reads=(f"bw_in{gi // 8}",), writes=(buf.name,),
                   ch="cwa" if gi % 2 == 0 else "cwb")
            return buf

        stream_in(0)
        stream_in(1)
        conv_par = [0]
        for oc in range(80):
            gi = oc // 2
            buf = wbufA if gi % 2 == 0 else wbufB
            off = (oc % 2) * 2048
            bank = oc % 2
            for dc in range(16):
                mm(bank, buf[:, off + dc * 128:off + (dc + 1) * 128],
                   fmbuf[:, dc * 512:(dc + 1) * 512],
                   start=(dc == 0), stop=(dc == 15),
                   rkeys=(buf.name, "fmbuf"))
            if oc < 32:
                zs = zstage0 if oc % 2 == 0 else zstage1
                evac(zs[:], pb[bank][:], (f"pb{bank}",), (zs.name,))
                tk.dma("qact", zscr[oc * 128:(oc + 1) * 128, :], zs[:],
                       reads=(zs.name,), writes=("zscr",), ch="czs")
            else:
                c48 = oc - 32
                xs = xbcs0 if c48 % 2 == 0 else xbcs1
                ca = convacc0 if c48 % 2 == 0 else convacc1
                tk.op("dve", lambda e, x_=xs: e.tensor_copy(x_[:, 3:515], pb[bank][:]),
                      reads=(f"pb{bank}",), writes=(xs.name,))
                tk.op("dve", lambda e, x_=xs: e.tensor_copy(
                    x_[:, 0:3], tail[:, c48 * 3:c48 * 3 + 3]),
                    reads=("tail",), writes=(xs.name,))
                tk.op("dve", lambda e, x_=xs: e.tensor_copy(
                    tail[:, c48 * 3:c48 * 3 + 3], x_[:, 512:515]),
                    reads=(xs.name,), writes=("tail",))
                ceng = "dve"
                conv_par[0] += 1
                tk.op(ceng, lambda e, x_=xs, a_=ca: e.tensor_scalar(
                    a_[:], x_[:, 0:512], c_convw[:, c48 * 4:c48 * 4 + 1],
                    c_convb[:, c48:c48 + 1], ALU.mult, ALU.add),
                    reads=(xs.name, "c_convw", "c_convb"), writes=(ca.name,))
                for kk in range(1, 4):
                    tk.op(ceng, lambda e, x_=xs, a_=ca, k_=kk: e.scalar_tensor_tensor(
                        a_[:], x_[:, k_:k_ + 512], c_convw[:, c48 * 4 + k_:c48 * 4 + k_ + 1],
                        a_[:], ALU.mult, ALU.add),
                        reads=(xs.name, "c_convw", ca.name), writes=(ca.name,))
                if c48 < 32:
                    dst = big1b[:, c48 * 512:(c48 + 1) * 512]
                    dk = "big1"
                elif c48 < 40:
                    g = c48 - 32
                    dst = b_fm[:, g * 512:(g + 1) * 512]
                    dk = "b_fm"
                else:
                    g = c48 - 40
                    dst = c_fm[:, g * 512:(g + 1) * 512]
                    dk = "c_fm"
                tk.op("act", lambda e, d_=dst, a_=ca: e.activation(d_, a_[:], AF.Silu),
                      reads=(ca.name,), writes=(dk,))
            if oc % 2 == 1 and gi + 2 < 40:
                stream_in(gi + 2)
        tk.dma("qsp", wbufA[:, 0:1024], bw_indt[:, :], reads=("bw_indt",),
               writes=("wbufA",), ch="cwa")
        for dc in range(16):
            mm(2, wbufA[:, dc * 64:(dc + 1) * 64], fmbuf[:, dc * 512:(dc + 1) * 512],
               start=(dc == 0), stop=(dc == 15), rkeys=("wbufA", "fmbuf"),
               out_sl=pb[2][0:64, :])
        tk.op("dve", lambda e: e.tensor_scalar(dts[:], pb[2][0:64, :], c_dtb[:],
                                               None, ALU.add),
              reads=("pb2", "c_dtb"), writes=("dts",))
        dtab = big3[0:64, 2048:2560]
        tk.op("act", lambda e: e.activation(dtab, dts[:], AF.Abs),
              reads=("dts",), writes=("big3",))
        tk.op("act", lambda e: e.activation(dtab, dtab, AF.Exp, scale=-1.0),
              reads=("big3",), writes=("big3",))
        tk.op("act", lambda e: e.activation(dtab, dtab, AF.Ln, bias=1.0),
              reads=("big3",), writes=("big3",))
        tk.op("dve", lambda e: e.tensor_scalar_max(dts[:], dts[:], 0.0),
              reads=("dts",), writes=("dts",))
        tk.op("dve", lambda e: e.tensor_add(dts[:], dts[:], dtab),
              reads=("dts", "big3"), writes=("dts",))
        if t == 0:
            tap("z0", zscr, (32 * 128, 512), BF16)
            tap("xfm0", big1b[:, 0:16384], (128, 16384), BF16)
            tap("bfm0", b_fm[:], (128, 4096), BF16)
            tap("cfm0", c_fm[:], (128, 4096), BF16)
            tap("dts0", dts[:], (64, 512), F32)
        if stage == "inproj":
            break

        for cc in range(2):
            csl = slice(cc * 256, (cc + 1) * 256)
            tk.op("dve", lambda e: e.tensor_scalar(dA, dts[:, csl], c_acol[0:64, :],
                                                   None, ALU.mult),
                  reads=("dts", "c_acol"), writes=("big3",))
            tk.op("dve", lambda e: e.tensor_tensor_scan(
                dAcs[:], dA, c_zcol[0:64, 0:1].broadcast_to([64, 256]), 0.0,
                ALU.add, ALU.add),
                reads=("big3", "c_zcol"), writes=("dAcs",))
            tk.op("act", lambda e: e.activation(lndt, dts[:, csl], AF.Ln),
                  reads=("dts",), writes=("big3",))
            tk.op("dve", lambda e: e.tensor_sub(lndt, lndt, dAcs[:]),
                  reads=("big3", "dAcs"), writes=("big3",))
            tk.op("act", lambda e: e.activation(w_hm, dAcs[:], AF.Exp, scale=-1.0,
                                                bias=dAcs[:, 255:256]),
                  reads=("dAcs",), writes=("ytmp",))
            tk.op("dve", lambda e: e.tensor_mul(w_hm, w_hm, dts[:, csl]),
                  reads=("ytmp", "dts"), writes=("ytmp",))
            for (srcT, dstT, skey, key) in (
                    (dAcs[:], dAcsT, "dAcs", "dAcsT"),
                    (lndt, colpT, "big3", "colpT"),
                    (w_hm, wT, "ytmp", "wT")):
                for jc in range(2):
                    tp(7, srcT[:, jc * 128:(jc + 1) * 128], c_id[0:64, 0:64],
                       (skey,), pb[7][0:128, 0:64])
                    evac(dstT[:, jc * 64:(jc + 1) * 64], pb[7][0:128, 0:64],
                         ("pb7",), (key,))
            tp(7, dAcs[:, 255:256], c_id[0:64, 0:64], ("dAcs",), pb[7][0:1, 0:64])
            tk.op("dve", lambda e: e.tensor_copy(decrow[:], pb[7][0:1, 0:64]),
                  reads=("pb7",), writes=("decrow",))
            tk.op("act", lambda e: e.activation(decrow[:], decrow[:], AF.Exp),
                  reads=("decrow",), writes=("decrow",))
            tk.op("pe", lambda e: e.matmul(pb[7][0:128, 0:64], c_ones1f[:], decrow[:],
                                           start=True, stop=True),
                  reads=("c_ones1f", "decrow"), writes=("pb7",))
            tk.op("dve", lambda e: e.tensor_copy(decayexp[:], pb[7][0:128, 0:64]),
                  reads=("pb7",), writes=("decayexp",))
            for jc in range(2):
                for half in range(4):
                    bk = 6 if (jc * 4 + half) % 2 == 0 else 7
                    for k in range(8):
                        c = half * 8 + k
                        tp(bk, big1b[:, c * 512 + cc * 256 + jc * 128:
                                     c * 512 + cc * 256 + (jc + 1) * 128],
                           c_idb[:], ("big1",), pbb[bk][:, k * 128:(k + 1) * 128])
                    dst = fmbuf[:, jc * 4096 + half * 1024:jc * 4096 + (half + 1) * 1024]
                    tk.op("dve", lambda e, d_=dst, b_=bk: e.tensor_copy(d_, pbb[b_][:, 0:1024]),
                          reads=(f"pb{bk}",), writes=("fmbuf",))

            for g in range(NG):
                gsl = slice(g * 512, (g + 1) * 512)
                dAcs8 = ytmp[0:8, 0:256]
                for jc in range(2):
                    tp(5, dAcsT[:, jc * 64 + g * 8:jc * 64 + (g + 1) * 8],
                       c_id[:], ("dAcsT",), pb[5][0:8, 0:128])
                    evac(dAcs8[:, jc * 128:(jc + 1) * 128], pb[5][0:8, 0:128],
                         ("pb5",), ("ytmp",))
                tk.op("act", lambda e: e.activation(e8g[:], dAcs8, AF.Exp),
                      reads=("ytmp",), writes=("e8g",))
                dup = lambda ap: ap.unsqueeze(1).broadcast_to([8, 2, 256])
                tk.op("dve", lambda e: e.tensor_copy(
                    dAcs8h[:].rearrange("p (r i) -> p r i", r=2), dup(dAcs8)),
                    reads=("ytmp",), writes=("dAcs8h",))
                tk.op("dve", lambda e: e.tensor_sub(dAcs8, dAcs8, dAcs8h[:, 0:256]),
                      reads=("ytmp", "dAcs8h"), writes=("ytmp",))
                tk.op("dve", lambda e: e.tensor_copy(
                    dAcs8m[:].rearrange("p (r i) -> p r i", r=2), dup(dAcs8)),
                    reads=("ytmp",), writes=("dAcs8m",))
                tk.op("dve", lambda e: e.tensor_sub(dAcs8, dAcs8, dAcs8m[:, 0:256]),
                      reads=("ytmp", "dAcs8m"), writes=("ytmp",))
                tk.op("dve", lambda e: e.tensor_copy(
                    dAcs8l[:].rearrange("p (r i) -> p r i", r=2), dup(dAcs8)),
                    reads=("ytmp",), writes=("dAcs8l",))
                tk.op("dve", lambda e: e.tensor_copy(sbf_g[:], S[:, gsl]),
                      reads=("S",), writes=("sbf_g",))
                for jc in range(2):
                    mm(5, b_fm[:, g * 512 + cc * 256 + jc * 128:
                               g * 512 + cc * 256 + (jc + 1) * 128],
                       c_fm[:, g * 512 + csl.start:g * 512 + csl.stop],
                       start=True, stop=True, rkeys=("b_fm", "c_fm"),
                       out_sl=pb[5][0:128, 0:256])
                    evac(cbT[:, jc * 256:(jc + 1) * 256], pb[5][0:128, 0:256],
                         ("pb5",), ("cbT",))
                for jc in range(2):
                    tp(5, b_fm[:, g * 512 + cc * 256 + jc * 128:
                               g * 512 + cc * 256 + (jc + 1) * 128],
                       c_idb[:], ("b_fm",), pbb[5][:, 0:128])
                    evac(bT[:, jc * 128:(jc + 1) * 128], pbb[5][:, 0:128],
                         ("pb5",), ("bT",))
                EBUF = [(expb0[:], "expb0"), (expb1[:], "expb1"),
                        (big2b[:, 1024:1280], "big2"), (big2b[:, 1280:1536], "big2")]
                MBUF = [(mT0[:], "mT0"), (mT1[:], "mT1"),
                        (big2b[:, 1536:1792], "big2"), (big2b[:, 1792:2048], "big2")]

                def emit_T1(hh):
                    t1bank = 0 if hh % 2 == 0 else 1
                    for i3, split in enumerate((dAcs8h, dAcs8m, dAcs8l)):
                        mm(t1bank, c_sel8[:, hh * 128:(hh + 1) * 128], split[:],
                           start=(i3 == 0), stop=False,
                           rkeys=("c_sel8", split.name),
                           out_sl=pb[t1bank][0:128, 0:512])
                    mm(t1bank, c_idb[:], c_mask[:],
                       start=False, stop=True, rkeys=("c_idb", "c_mask"),
                       out_sl=pb[t1bank][0:128, 0:512])

                emit_T1(0)
                for rc in range(4):
                    ydbank = 2 if rc % 2 == 0 else 3
                    for sub in range(2):
                        hh = 2 * rc + sub
                        h = g * 8 + hh
                        t1bank = 0 if hh % 2 == 0 else 1
                        if hh + 1 < 8:
                            emit_T1(hh + 1)
                        par = hh % 2
                        for jc in range(2):
                            eb, ebk = EBUF[2 * par + jc]
                            mt, mtk = MBUF[2 * par + jc]
                            tk.op("act", lambda e, x_=eb, tb=t1bank, j_=jc, h_=h:
                                  e.activation(x_, pb[tb][0:128, j_ * 256:(j_ + 1) * 256],
                                               AF.Exp,
                                               bias=colpT[:, j_ * 64 + h_:j_ * 64 + h_ + 1]),
                                  reads=(f"pb{t1bank}", "colpT"), writes=(ebk,))
                            tk.op("dve", lambda e, x_=eb, m_=mt, j_=jc:
                                  e.tensor_mul(m_, x_,
                                               cbT[:, j_ * 256:(j_ + 1) * 256]),
                                  reads=(ebk, "cbT"), writes=(mtk,))
                            mm(ydbank, fmbuf[:, jc * 4096 + h * 64:jc * 4096 + (h + 1) * 64],
                               mt, start=(jc == 0), stop=(jc == 1),
                               rkeys=("fmbuf", mtk),
                               out_sl=pb[ydbank][sub * 64:(sub + 1) * 64, 0:256])
                    yobank = 4 if rc % 2 == 0 else 6
                    mm(yobank, sbf_g[:, rc * 128:(rc + 1) * 128],
                       c_fm[:, g * 512 + csl.start:g * 512 + csl.stop],
                       start=True, stop=True, rkeys=("sbf_g", "c_fm"),
                       out_sl=pb[yobank][0:128, 0:256])
                    mm(7, c_selpair[:, rc * 128:(rc + 1) * 128], e8g[:],
                       start=True, stop=True, rkeys=("c_selpair", "e8g"),
                       out_sl=pb[7][0:128, 0:256])
                    tk.op("dve", lambda e: e.tensor_copy(eb_s[:], pb[7][0:128, 0:256]),
                          reads=("pb7",), writes=("eb_s",))
                    tk.op("dve", lambda e, yb=yobank: e.tensor_mul(
                        ytmp[:], pb[yb][0:128, 0:256], eb_s[:]),
                        reads=(f"pb{yobank}", "eb_s"), writes=("ytmp",))
                    tk.op("dve", lambda e, yd=ydbank: e.tensor_add(
                        ytmp[:], ytmp[:], pb[yd][0:128, 0:256]),
                        reads=("ytmp", f"pb{ydbank}"), writes=("ytmp",))
                    ych = g * 4 + rc
                    ydst = big3b[:, ych * 256:(ych + 1) * 256]
                    tk.op("dve", lambda e, d_=ydst: e.scalar_tensor_tensor(
                        d_, big1b[:, ych * 512 + csl.start:ych * 512 + csl.stop],
                        c_dexp[:, g * 4 + rc:g * 4 + rc + 1], ytmp[:],
                        ALU.mult, ALU.add),
                        reads=("big1", "c_dexp", "ytmp"), writes=("big3",))
                for jc in range(2):
                    src3 = fmbuf[:, jc * 4096 + gsl.start:jc * 4096 + gsl.stop]
                    tk.op("dve", lambda e, s_=src3, j_=jc: e.tensor_tensor(
                        wx[:, j_ * 512:(j_ + 1) * 512].rearrange("p (h q) -> p h q", q=64),
                        s_.rearrange("p (h q) -> p h q", q=64),
                        wT[:, j_ * 64 + g * 8:j_ * 64 + (g + 1) * 8].unsqueeze(2)
                        .broadcast_to([128, 8, 64]),
                        ALU.mult),
                        reads=("fmbuf", "wT"), writes=("wx",))
                for jc in range(2):
                    mm(5, bT[:, jc * 128:(jc + 1) * 128], wx[:, jc * 512:(jc + 1) * 512],
                       start=(jc == 0), stop=(jc == 1), rkeys=("bT", "wx"))
                tk.op("dve", lambda e: e.tensor_tensor(
                    S[:, gsl].rearrange("p (h q) -> p h q", q=64),
                    S[:, gsl].rearrange("p (h q) -> p h q", q=64),
                    decayexp[:, g * 8:(g + 1) * 8].unsqueeze(2).broadcast_to([128, 8, 64]),
                    ALU.mult),
                    reads=("S", "decayexp"), writes=("S",))
                tk.op("dve", lambda e: e.tensor_add(S[:, gsl], S[:, gsl], pb[5][:]),
                      reads=("S", "pb5"), writes=("S",))

            if t == 0 and cc == 0:
                tap("y0", big3b[:, 0:8192], (128, 8192), BF16)

            def zload(cg, buf):
                tk.dma("qact", buf[:].rearrange("p (o i) -> p o i", i=256),
                       zscr[:].rearrange("(o p) t -> p o t", p=128)[
                           :, cg * 2:(cg + 1) * 2, csl],
                       reads=("zscr",), writes=(buf.name,),
                       ch="cza" if buf is zg4a else "czb")

            zload(0, zg4a)
            zload(1, zg4b)
            for c in range(32):
                ys = ysq0 if c % 2 == 0 else ysq1
                tk.op("act", lambda e, y_=ys, c_=c: e.activation(
                    y_[:], big3b[:, c_ * 256:(c_ + 1) * 256], AF.Square),
                    reads=("big3",), writes=(ys.name,))
                mm(0, c_onescolb[:], ys[:], start=(c == 0), stop=(c == 31),
                   rkeys=("c_onescolb", ys.name), out_sl=pb[0][0:1, 0:256])
            tk.op("dve", lambda e: e.tensor_scalar_mul(varrow[:], pb[0][0:1, 0:256],
                                                       1.0 / 4096),
                  reads=("pb0",), writes=("varrow",))
            tk.op("dve", lambda e: e.tensor_scalar(varrow[:], varrow[:],
                                                   c_eps[0:1, :], None, ALU.add),
                  reads=("varrow", "c_eps"), writes=("varrow",))
            tk.op("act", lambda e: e.activation(varrow[:], varrow[:], AF.Ln),
                  reads=("varrow",), writes=("varrow",))
            tk.op("act", lambda e: e.activation(rstdb[:], varrow[:], AF.Exp, scale=-0.5),
                  reads=("varrow",), writes=("rstdb",))
            tk.op("pe", lambda e: e.matmul(pb[1][0:128, 0:256], c_ones1b[:], rstdb[:],
                                           start=True, stop=True),
                  reads=("c_ones1b", "rstdb"), writes=("pb1",))
            tk.op("dve", lambda e: e.tensor_copy(rb_s[:], pb[1][0:128, 0:256]),
                  reads=("pb1",), writes=("rb_s",))
            for c in range(32):
                cg = c // 2
                zbuf = zg4a if cg % 2 == 0 else zg4b
                ys = ysq0 if c % 2 == 0 else ysq1
                tk.op("dve", lambda e, c_=c: e.tensor_mul(
                    ytmp[:], big3b[:, c_ * 256:(c_ + 1) * 256], rb_s[:]),
                    reads=("big3", "rb_s"), writes=("ytmp",))
                tk.op("act", lambda e, y_=ys, z_=zbuf, c_=c: e.activation(
                    y_[:], z_[:, (c_ % 2) * 256:(c_ % 2 + 1) * 256], AF.Silu),
                    reads=(zbuf.name,), writes=(ys.name,))
                tk.op("dve", lambda e, y_=ys, c_=c: e.scalar_tensor_tensor(
                    big3b[:, 8192 + c_ * 256:8192 + (c_ + 1) * 256], ytmp[:],
                    c_normw[:, c_:c_ + 1], y_[:], ALU.mult, ALU.mult),
                    reads=("ytmp", "c_normw", ys.name, "big3"), writes=(f"yn{c}",))
                if c % 2 == 1 and cg + 2 < 16:
                    zload(cg + 2, zbuf)
            if t == 0 and cc == 0:
                tap("yn0", big3b[:, 8192:16384], (128, 8192), BF16)
            obuf = {0: None, 1: None}

            def stream_out(oc):
                buf = wbufA if oc % 2 == 0 else wbufB
                tk.dma("qsp", buf[:], bw_out[oc * 128:(oc + 1) * 128, :],
                       reads=(f"bw_out{oc // 8}",), writes=(buf.name,),
                       ch="cwa" if oc % 2 == 0 else "cwb")
                return buf

            obuf[0] = stream_out(0)
            obuf[1] = stream_out(1)
            for oc in range(16):
                buf = obuf[oc % 2]
                bank = 4 if oc % 2 == 0 else 6
                for ic in range(32):
                    mm(bank, buf[:, ic * 128:(ic + 1) * 128],
                       big3b[:, 8192 + ic * 256:8192 + (ic + 1) * 256],
                       start=(ic == 0), stop=(ic == 31),
                       rkeys=(buf.name, f"yn{ic}"), out_sl=pb[bank][0:128, 0:256])
                if oc + 2 < 16:
                    obuf[oc % 2] = stream_out(oc + 2)
                osb = os0 if oc % 2 == 0 else os1
                evac(osb[:], pb[bank][0:128, 0:256], (f"pb{bank}",), (osb.name,))
                slot = oc % 8
                for bb in range(2):
                    tb = 5 if bb == 0 else 7
                    tp(tb, osb[:, bb * 128:(bb + 1) * 128], c_idb[:],
                       (osb.name,), pbb[tb][:, slot * 128:(slot + 1) * 128])
                if oc % 8 == 7:
                    oc0 = oc - 7
                    for bb in range(2):
                        tb = 5 if bb == 0 else 7
                        b = 2 * cc + bb
                        dst = x_tm[:, b * 2048 + oc0 * 128:b * 2048 + (oc0 + 8) * 128]
                        tk.op("dve", lambda e, d_=dst, tb_=tb: e.tensor_add(
                            d_, d_, pbb[tb_][:, 0:1024]),
                            reads=("x_tm", f"pb{tb}"), writes=("x_tm",))
        if t == 0:
            tap("x1t0", x_tm[:], (128, 8192), F32)

        tk.alias("big3", "yn0", "yn1", "yn2", "yn3", "yn4", "yn5", "yn6", "yn7", "yn8", "yn9", "yn10", "yn11", "yn12", "yn13", "yn14", "yn15", "yn16", "yn17", "yn18", "yn19", "yn20", "yn21", "yn22", "yn23", "yn24", "yn25", "yn26", "yn27", "yn28", "yn29", "yn30", "yn31")
        for b in range(4):
            layernorm_block(b, sqv, "big3")
            transpose_to_fm(None, b, tbank=6)
        for b in range(4):
            for dc in range(16):
                mm(0, fmbuf[:, dc * 512 + b * 128:dc * 512 + (b + 1) * 128],
                   c_gateb[:, dc * 8:(dc + 1) * 8],
                   start=(dc == 0), stop=(dc == 15), rkeys=("fmbuf", "c_gateb"),
                   out_sl=pb[0][0:128, 0:8])
            tk.op("dve", lambda e: e.tensor_copy(glog[:], pb[0][0:128, 0:8]),
                  reads=("pb0",), writes=("glog",))
            tk.op("dve", lambda e: e.reduce_max(gm1[:], glog[:], AX.X),
                  reads=("glog",), writes=("gm1",))
            tk.op("dve", lambda e: e.tensor_scalar_mul(gm1[:], gm1[:], -1.0),
                  reads=("gm1",), writes=("gm1",))
            tk.op("act", lambda e: e.activation(gsc[:], glog[:], AF.Exp, bias=gm1[:]),
                  reads=("glog", "gm1"), writes=("gsc",))
            tk.op("dve", lambda e: e.reduce_sum(gm1[:], gsc[:], AX.X),
                  reads=("gsc",), writes=("gm1",))
            tk.op("dve", lambda e: e.reciprocal(gm1[:], gm1[:]),
                  reads=("gm1",), writes=("gm1",))
            tk.op("dve", lambda e: e.tensor_scalar(gsc[:], gsc[:], gm1[:], None,
                                                   ALU.mult),
                  reads=("gsc", "gm1"), writes=("gsc",))
            tk.op("dve", lambda e: e.reduce_max(gm1[:], gsc[:], AX.X),
                  reads=("gsc",), writes=("gm1",))
            tk.op("dve", lambda e: e.tensor_scalar(gtmp[:], gsc[:], gm1[:], None,
                                                   ALU.is_ge),
                  reads=("gsc", "gm1"), writes=("gtmp",))
            tk.op("dve", lambda e: e.scalar_tensor_tensor(gtmp[:], gtmp[:], -1e30,
                                                          gsc[:], ALU.mult, ALU.add),
                  reads=("gtmp", "gsc"), writes=("gtmp",))
            tk.op("dve", lambda e: e.reduce_max(gm1[:], gtmp[:], AX.X),
                  reads=("gtmp",), writes=("gm1",))
            tk.op("dve", lambda e: e.tensor_scalar(gtmp[:], gsc[:], gm1[:], None,
                                                   ALU.is_ge),
                  reads=("gsc", "gm1"), writes=("gtmp",))
            tk.op("dve", lambda e: e.tensor_mul(gsc[:], gsc[:], gtmp[:]),
                  reads=("gsc", "gtmp"), writes=("gsc",))
            tp(1, gsc[:, 0:8], c_id[:], ("gsc",), pb[1][0:8, 0:128])
            tk.op("dve", lambda e, b_=b: e.tensor_copy(
                comb_fm[:, b_ * 128:(b_ + 1) * 128], pb[1][0:8, 0:128]),
                reads=("pb1",), writes=("comb_fm",))
        if t == 0:
            tap("comb0", comb_fm[:], (8, 512), BF16)

        g_s = big2b[:, 0:5632]
        wdA = big3b[:, 0:5632]
        wdB = big3b[:, 5632:11264]
        tk.alias("wdA", "big3", "yn0", "yn1", "yn2", "yn3", "yn4", "yn5", "yn6", "yn7", "yn8", "yn9", "yn10", "yn11", "yn12", "yn13", "yn14", "yn15", "yn16", "yn17", "yn18", "yn19", "yn20", "yn21", "yn22", "yn23", "yn24", "yn25", "yn26", "yn27", "yn28", "yn29", "yn30", "yn31")
        tk.alias("wdB", "big3", "yn0", "yn1", "yn2", "yn3", "yn4", "yn5", "yn6", "yn7", "yn8", "yn9", "yn10", "yn11", "yn12", "yn13", "yn14", "yn15", "yn16", "yn17", "yn18", "yn19", "yn20", "yn21", "yn22", "yn23", "yn24", "yn25", "yn26", "yn27", "yn28", "yn29", "yn30", "yn31")

        def up_stage(hc, buf, gb, ub, comb_bank=None):
            for dc in range(16):
                mm(gb, buf[:, dc * 128:(dc + 1) * 128],
                   fmbuf[:, dc * 512:(dc + 1) * 512], start=(dc == 0),
                   stop=(dc == 15), rkeys=(buf.name, "fmbuf"))
            for dc in range(16):
                mm(ub, buf[:, 2048 + dc * 128:2048 + (dc + 1) * 128],
                   fmbuf[:, dc * 512:(dc + 1) * 512], start=(dc == 0),
                   stop=(dc == 15), rkeys=(buf.name, "fmbuf"))
            sl = hc % 11
            zz = zstage0 if hc % 2 == 0 else zstage1
            tk.op("act", lambda e, z_=zz, g_=gb: e.activation(z_[:], pb[g_][:], AF.Silu),
                  reads=(f"pb{gb}",), writes=(zz.name,))
            tk.op("dve", lambda e, z_=zz, u_=ub, s_=sl: e.tensor_tensor(
                g_s[:, s_ * 512:(s_ + 1) * 512], z_[:], pb[u_][:], ALU.mult),
                reads=(zz.name, f"pb{ub}"), writes=("big2",))
            if comb_bank is not None:
                tk.op("dve", lambda e, s_=sl: e.tensor_tensor(
                    g_s[:, s_ * 512:(s_ + 1) * 512],
                    g_s[:, s_ * 512:(s_ + 1) * 512],
                    pb[comb_bank][:], ALU.mult),
                    reads=("big2", f"pb{comb_bank}"), writes=("big2",))

        def sgu_dma(hc, buf):
            tk.dma("qsp", buf[:], bw_sgu[hc * 128:(hc + 1) * 128, :],
                   reads=(f"bw_sgu{hc // 11}",), writes=(buf.name,),
                   ch="cwa" if buf is wbufA else "cwb")

        def sd_dma(dg, buf, bkey):
            tk.dma("qsp", buf, bw_sd[dg * 128:(dg + 1) * 128, :],
                   reads=(f"bw_sd{dg // 4}",), writes=(bkey,),
                   ch="cda" if bkey == "wdA" else "cdb")

        bufs2 = [wbufA, wbufB]
        for sp_pass in range(2):
            h0 = sp_pass * 11
            sgu_dma(h0 + 0, bufs2[0])
            sgu_dma(h0 + 1, bufs2[1])
            for hl in range(11):
                hc = h0 + hl
                buf = bufs2[hl % 2]
                gb = 2 if hl % 2 == 0 else 4
                ub = 3 if hl % 2 == 0 else 6
                up_stage(hc, buf, gb, ub)
                if hl + 2 < 11:
                    sgu_dma(h0 + hl + 2, bufs2[hl % 2])
            sd_dma(0, wdA, "wdA")
            sd_dma(1, wdB, "wdB")
            for dg in range(8):
                buf, bkey = (wdA, "wdA") if dg % 2 == 0 else (wdB, "wdB")
                for sub in range(2):
                    dc = dg * 2 + sub
                    bank = 0 if dc % 2 == 0 else 1
                    for hl in range(11):
                        hc = h0 + hl
                        mm(bank, buf[:, sub * 2816 + hc * 128:sub * 2816 + (hc + 1) * 128],
                           g_s[:, hl * 512:(hl + 1) * 512], start=(hl == 0),
                           stop=(hl == 10), rkeys=(bkey, "big2"))
                    if sp_pass == 0:
                        tk.op("dve", lambda e, dc_=dc, b_=bank: e.tensor_copy(
                            big1[:, dc_ * 512:(dc_ + 1) * 512], pb[b_][:]),
                            reads=(f"pb{bank}",), writes=("big1",))
                    else:
                        tk.op("dve", lambda e, dc_=dc, b_=bank: e.tensor_add(
                            big1[:, dc_ * 512:(dc_ + 1) * 512],
                            big1[:, dc_ * 512:(dc_ + 1) * 512], pb[b_][:]),
                            reads=("big1", f"pb{bank}"), writes=("big1",))
                if dg + 2 < 8:
                    sd_dma(dg + 2, buf, bkey)

        def gu_dma(ex_, hc, buf):
            row0 = (ex_ * 11 + hc) * 128
            tk.dma("qsp", buf[:], bw_gu[row0:row0 + 128, :],
                   reads=(f"bw_gu{ex_}",), writes=(buf.name,),
                   ch="cwa" if buf is wbufA else "cwb")

        def d_dma(ex_, dg, buf, bkey):
            row0 = (ex_ * 4 + dg) * 128
            tk.dma("qsp", buf, bw_d[row0:row0 + 128, :],
                   reads=(f"bw_d{ex_}",), writes=(bkey,),
                   ch="cda" if bkey == "wdA" else "cdb")

        for ex in range(EXP):
            tk.op("pe", lambda e, x_=ex: e.matmul(
                pb[7][0:128, 0:512], c_sel8[:, x_ * 128:(x_ + 1) * 128], comb_fm[:],
                start=True, stop=True),
                reads=("c_sel8", "comb_fm"), writes=("pb7",))
            gu_dma(ex, 0, wbufA)
            gu_dma(ex, 1, wbufB)
            for hc in range(11):
                buf = bufs2[hc % 2]
                gb = 2 if hc % 2 == 0 else 4
                ub = 3 if hc % 2 == 0 else 6
                up_stage(hc, buf, gb, ub, comb_bank=7)
                if hc + 2 < 11:
                    gu_dma(ex, hc + 2, bufs2[hc % 2])
            d_dma(ex, 0, wdA, "wdA")
            d_dma(ex, 1, wdB, "wdB")
            for dg in range(4):
                buf, bkey = (wdA, "wdA") if dg % 2 == 0 else (wdB, "wdB")
                for sub in range(4):
                    dc = dg * 4 + sub
                    bank = 0 if dc % 2 == 0 else 1
                    for hc in range(11):
                        mm(bank, buf[:, sub * 1408 + hc * 128:sub * 1408 + (hc + 1) * 128],
                           g_s[:, hc * 512:(hc + 1) * 512], start=(hc == 0),
                           stop=(hc == 10), rkeys=(bkey, "big2"))
                    tk.op("dve", lambda e, dc_=dc, b_=bank: e.tensor_add(
                        big1[:, dc_ * 512:(dc_ + 1) * 512],
                        big1[:, dc_ * 512:(dc_ + 1) * 512], pb[b_][:]),
                        reads=("big1", f"pb{bank}"), writes=("big1",))
                if dg + 2 < 4:
                    d_dma(ex, dg + 2, buf, bkey)

        for b in range(4):
            for half in range(4):
                for k in range(4):
                    dc = half * 4 + k
                    tp(5, big1[:, dc * 512 + b * 128:dc * 512 + (b + 1) * 128],
                       c_id[:], ("big1",), pb[5][:, k * 128:(k + 1) * 128])
                dst = x_tm[:, b * 2048 + half * 512:b * 2048 + (half + 1) * 512]
                tk.op("dve", lambda e, d_=dst: e.tensor_add(d_, d_, pb[5][:]),
                      reads=("x_tm", "pb5"), writes=("x_tm",))
            tk.dma("qact", out[T0 + b * 128:T0 + (b + 1) * 128, :],
                   x_tm[:, b * 2048:(b + 1) * 2048],
                   reads=("x_tm",), writes=("out",), ch="cout")

    tk.finish()
    nc.finalize()
    return nc



_CACHE = {}


BUILD_TILES = N_TILES
BUILD_STAGE = "full"


def kernel(**inputs):
    const, xs = prep_inputs(inputs)
    nc = build_program(BUILD_TILES, tuple(DEBUG_TAPS), BUILD_STAGE)
    in_maps = []
    for c in range(8):
        m = dict(const)
        m["xin"] = xs[c]
        in_maps.append(m)
    res = run_bass_kernel_spmd(nc, in_maps, core_ids=list(range(8)))
    outs = np.stack([res.results[c]["out"] for c in range(8)], axis=0)
    if DEBUG_TAPS:
        kernel.taps = [{k.replace("tap_", ""): v for k, v in r.items()
                        if k.startswith("tap_")} for r in res.results]
    return outs
